# version 01
# baseline (speedup 1.0000x reference)
import math

import numpy as np

import concourse.bass as bass
import concourse.mybir as mybir
from concourse.bass_utils import run_bass_kernel_spmd

F32 = mybir.dt.float32
AF = mybir.ActivationFunctionType
ALU = mybir.AluOpType
AX = mybir.AxisListType

B, S, D = 16, 2048, 1024
NCORE = 8
BPC = B // NCORE
NCH = D // 128
HEADS, DH = 16, 64
LN_EPS = 1e-5
RMS_EPS = 1e-6
L1LN_EPS = 1e-5
CHUNK = 64
N_CHUNKS = S // CHUNK
MAX_LR, MIN_DECAY, MAX_NORM = 0.2, 0.5, 30.0
KEY_EPS = 1e-5
NEXP = 8
FF = 2 * D


def _host_bucket(rel_pos):
    NUM_BUCKETS, MAX_DISTANCE = 32, 128
    n = -rel_pos
    nb = NUM_BUCKETS // 2
    ret = (n < 0).astype(np.int32) * nb
    n = np.abs(n)
    max_exact = nb // 2
    is_small = n < max_exact
    val_if_large = max_exact + (
        np.log(np.maximum(n, 1).astype(np.float32) / max_exact)
        / math.log(MAX_DISTANCE / max_exact) * (nb - max_exact)
    ).astype(np.int32)
    val_if_large = np.minimum(val_if_large, nb - 1)
    return ret + np.where(is_small, n, val_if_large)


def _host_band16(rel_emb):
    s = np.arange(4095, dtype=np.int64)
    bucket = _host_bucket(s - 2047)
    return np.ascontiguousarray(np.asarray(rel_emb)[bucket, :].T)


SEM_NAMES = (["tick", "wfree", "gsem", "mmS", "evS", "vS", "scS",
              "x1", "x2", "x3", "x4"] + [f"ws{k}" for k in range(16)]
             + [f"sl{k}" for k in range(8)])
NWS = 16
WCOLS = 12288


class Ctx:
    def __init__(self, name):
        self.nc = bass.Bass("TRN2", target_bir_lowering=False)
        self.name = name
        self._exits = []
        self.cnt = {}
        self.hbm = {}
        self._inced = set()

    def _enter(self, cm):
        v = cm.__enter__()
        self._exits.append(cm)
        return v

    def alloc(self):
        nc = self.nc
        self.A = self._enter(nc.sbuf_tensor("bufA", [128, 16384], F32))
        self.Bb = self._enter(nc.sbuf_tensor("bufB", [128, 9216], F32))
        self.C = self._enter(nc.sbuf_tensor("bufC", [128, 8192], F32))
        self.W = self._enter(nc.sbuf_tensor("wring", [128, WCOLS], F32))
        self.SM = self._enter(nc.sbuf_tensor("small", [128, 6656], F32))
        self.P0 = self._enter(nc.psum_tensor("ps0", [128, 2048], F32))
        self.P1 = self._enter(nc.psum_tensor("ps1", [128, 512], F32))
        self.P2 = self._enter(nc.psum_tensor("ps2", [128, 512], F32))
        self.P3 = self._enter(nc.psum_tensor("ps3", [128, 512], F32))
        self.P4 = self._enter(nc.psum_tensor("ps4", [128, 512], F32))
        for s in SEM_NAMES:
            setattr(self, s, self._enter(nc.semaphore(s)))
            self.cnt[s] = 0
        self.block = self._enter(nc.Block())
        self.ws = WStream(self)
        n = self.nc
        self.sy, self.te, self.ve, self.sc, self.gp = (
            n.sync, n.tensor, n.vector, n.scalar, n.gpsimd)

    def close(self):
        for cm in reversed(self._exits):
            cm.__exit__(None, None, None)

    def inc(self, instr, sem, n=1):
        instr.then_inc(getattr(self, sem), n)
        self.cnt[sem] += n
        return self.cnt[sem]

    def wait(self, eng, sem, val=None):
        eng.wait_ge(getattr(self, sem), self.cnt[sem] if val is None else val)

    def din(self, name, shape):
        if name in self.hbm:
            return self.hbm[name]
        ap = self.nc.dram_tensor(name, list(shape), F32, kind="ExternalInput").ap()
        self.hbm[name] = ap
        return ap

    def dout(self, name, shape):
        ap = self.nc.dram_tensor(name, list(shape), F32, kind="ExternalOutput").ap()
        self.hbm[name] = ap
        return ap

    def end_stage(self, engines):
        for e in engines:
            if e is self.gp:
                gp_wait_all(self)
                self.gp.sem_inc(self.tick, 1)
                self.cnt["tick"] += 1
            else:
                self.inc(e.drain(), "tick", 1)


class WStream:

    def __init__(self, ctx):
        self.ctx = ctx
        self.cols = WCOLS
        self.cur = 0
        self.fetches = []
        self.nf = 0
        self.rel_cnt = 0
        self.bounds = []

    def fetch(self, hbm_ap, ncols, rows=128):
        ctx = self.ctx
        assert ncols <= self.cols
        if self.cur + ncols > self.cols:
            self.cur = 0
        start, end = self.cur, self.cur + ncols
        need = 0
        for i, (s0, e0) in enumerate(self.fetches):
            if s0 < end and e0 > start:
                need = i + 1
        if need > 0:
            bnd = next(b for b in self.bounds if b >= need)
            ctx.sy.wait_ge(ctx.wfree, bnd)
        k = self.nf % NWS
        prior = self.nf // NWS
        sem = getattr(ctx, f"ws{k}")
        if prior > 0:
            ctx.sy.wait_ge(sem, 16 * prior)
        dst = ctx.W[0:rows, start:end]
        ctx.nc.sync.dma_start(out=dst, in_=hbm_ap).then_inc(sem, 16)
        self.nf += 1
        self.cur = end
        self.fetches.append((start, end))
        return dst, (k, 16 * (prior + 1))

    def rel(self, instr, n=1):
        instr.then_inc(self.ctx.wfree, n)
        self.rel_cnt += n
        self.bounds.append(self.rel_cnt)

    def rel_seq(self, eng, n):
        eng.wait_ge(self.ctx.mmS, self.ctx.cnt["mmS"])
        eng.sem_inc(self.ctx.wfree, n)
        self.rel_cnt += n
        self.bounds.append(self.rel_cnt)


class PsumPool:
    def __init__(self, ctx, banks):
        self.ctx = ctx
        self.banks = banks
        self.free_at = [None] * len(banks)
        self.i = 0

    def acquire(self, eng):
        i = self.i % len(self.banks)
        self.i += 1
        if self.free_at[i] is not None:
            sem, tgt = self.free_at[i]
            eng.wait_ge(getattr(self.ctx, sem), tgt)
        return i, self.banks[i]

    def release(self, i, sem, tgt):
        self.free_at[i] = (sem, tgt)


def wait_all(ctx, engs):
    for e in engs:
        ctx.wait(e, "tick")


def gfence(ctx):
    ctx.gp.wait_ge(ctx.gsem, ctx.cnt["gsem"])


def gp_wait_all(ctx):
    ctx.gp.wait_ge(ctx.gsem, ctx.cnt["gsem"])
    for k in range(8):
        if ctx.cnt[f"sl{k}"] > 0:
            ctx.gp.wait_ge(getattr(ctx, f"sl{k}"), ctx.cnt[f"sl{k}"])



def st_layernorm(ctx, src_name, dst_name, eps):
    te, ve, sc, gp = ctx.te, ctx.ve, ctx.sc, ctx.gp
    A, C, SM = ctx.A, ctx.C, ctx.SM
    P1, P2, P3, P4 = ctx.P1, ctx.P2, ctx.P3, ctx.P4
    src, dst = ctx.hbm[src_name], ctx.hbm[dst_name]
    wait_all(ctx, [te, ve, sc, gp])
    ctx.inc(gp.dma_start(out=SM[:, 0:128], in_=ctx.hbm["ones"][:, :]), "gsem", 16)
    gfence(ctx)
    ones_col = SM[:, 0:1]
    RB = 256
    for b in range(BPC):
        for c in range(NCH):
            ctx.inc(gp.dma_start(out=A[:, c * 2048:(c + 1) * 2048],
                                 in_=src[b, c * 128:(c + 1) * 128, :]), "gsem", 16)
        gfence(ctx)
        ld = ctx.cnt["gsem"]
        for tt in range(4):
            ve.wait_ge(ctx.gsem, ld)
            i = None
            for c in range(NCH):
                xs = A[:, c * 2048 + tt * 512: c * 2048 + tt * 512 + 512]
                i = ve.tensor_tensor(C[:, c * 512:(c + 1) * 512], xs, xs, op=ALU.mult)
            vtgt = ctx.inc(i, "vS", 1)
            te.wait_ge(ctx.gsem, ld)
            te.wait_ge(ctx.evS, ctx.cnt["evS"])
            for c in range(NCH):
                xs = A[:, c * 2048 + tt * 512: c * 2048 + tt * 512 + 512]
                te.matmul(P1[0:1, 0:512], ones_col, xs, start=(c == 0), stop=(c == NCH - 1))
            te.wait_ge(ctx.vS, vtgt)
            i = None
            for c in range(NCH):
                i = te.matmul(P2[0:1, 0:512], ones_col, C[:, c * 512:(c + 1) * 512],
                              start=(c == 0), stop=(c == NCH - 1))
            mtgt = ctx.inc(i, "mmS", 1)
            ve.wait_ge(ctx.mmS, mtgt)
            ve.tensor_copy(C[0:1, 4096 + tt * 512: 4096 + tt * 512 + 512], P1[0:1, 0:512])
            i = ve.tensor_copy(C[0:1, 6144 + tt * 512: 6144 + tt * 512 + 512], P2[0:1, 0:512])
            ctx.inc(i, "evS", 1)
        mu = C[0:1, 4096:6144]; sq = C[0:1, 6144:8192]
        rstd = SM[0:1, RB:RB + 2048]; m2 = SM[0:1, RB + 2048:RB + 4096]
        ctx.wait(ve, "evS")
        ve.drain()
        ve.tensor_scalar_mul(mu, mu, 1.0 / D)
        ve.drain()
        ve.tensor_tensor(m2, mu, mu, op=ALU.mult)
        ve.drain()
        ve.tensor_scalar_add(m2, m2, -float(eps))
        ve.drain()
        i = ve.scalar_tensor_tensor(rstd, sq, 1.0 / D, m2, op0=ALU.mult, op1=ALU.subtract)
        vtgt = ctx.inc(i, "vS", 1)
        sc.wait_ge(ctx.vS, vtgt)
        i = sc.activation(m2, rstd, AF.Sqrt)
        stgt = ctx.inc(i, "scS", 1)
        ve.wait_ge(ctx.scS, stgt)
        ve.reciprocal(rstd, m2)
        ve.drain()
        i = ve.tensor_tensor(m2, mu, rstd, op=ALU.mult)
        vtgt = ctx.inc(i, "vS", 1)
        st_fence = [0]
        for tt in range(4):
            te.wait_ge(ctx.vS, max(vtgt, ctx.cnt["vS"]))
            te.matmul(P3[:, 0:512], SM[0:1, 0:128], rstd[0:1, tt * 512:(tt + 1) * 512],
                      start=True, stop=True)
            i = te.matmul(P4[:, 0:512], SM[0:1, 0:128], m2[0:1, tt * 512:(tt + 1) * 512],
                          start=True, stop=True)
            mtgt = ctx.inc(i, "mmS", 1)
            ve.wait_ge(ctx.mmS, mtgt)
            if st_fence[0]:
                ve.wait_ge(ctx.gsem, st_fence[0])
            last = None
            for c in range(NCH):
                xs = A[:, c * 2048 + tt * 512: c * 2048 + tt * 512 + 512]
                ve.tensor_tensor(C[:, c * 512:(c + 1) * 512], xs, P3[:, 0:512], op=ALU.mult)
            ve.drain()
            for c in range(NCH):
                o = C[:, c * 512:(c + 1) * 512]
                last = ve.tensor_tensor(o, o, P4[:, 0:512], op=ALU.subtract)
            vtgt2 = ctx.inc(last, "vS", 1)
            gp.wait_ge(ctx.vS, vtgt2)
            for c in range(NCH):
                ctx.inc(gp.dma_start(out=dst[b, c * 128:(c + 1) * 128, tt * 512:(tt + 1) * 512],
                                     in_=C[:, c * 512:(c + 1) * 512]), "gsem", 16)
            gfence(ctx)
            st_fence[0] = ctx.cnt["gsem"]
        gp_wait_all(ctx)
        gp.sem_inc(ctx.x1, 1); ctx.cnt["x1"] += 1
        ve.wait_ge(ctx.x1, ctx.cnt["x1"])
        te.wait_ge(ctx.x1, ctx.cnt["x1"])
    ctx.end_stage([te, ve, sc, gp])


def mm_proj(ctx, *, nk, lhsT_fn, rhs_fn, evac_fn, of_list, tt_list, pool, out_rows=128):
    te = ctx.te
    waited = {}
    for tt in tt_list:
        for of in of_list:
            bi, ps = pool.acquire(te)
            pst = ps if out_rows == 128 else ps[0:out_rows, :]
            last = None
            for j in range(nk):
                lap, wv = lhsT_fn(j, of)
                if wv is not None and waited.get(wv[0], 0) < wv[1]:
                    te.wait_ge(getattr(ctx, f"ws{wv[0]}"), wv[1])
                    waited[wv[0]] = wv[1]
                last = te.matmul(pst, lap, rhs_fn(j, tt), start=(j == 0), stop=(j == nk - 1))
            mtgt = ctx.inc(last, "mmS", 1)
            sem, tgt = evac_fn(ps, of, tt, mtgt)
            pool.release(bi, sem, tgt)


def load_bias_chunks(ctx, name, n, sm_col):
    gp = ctx.gp
    ap = ctx.hbm[name]
    for c in range(n):
        ctx.inc(gp.dma_start(out=ctx.SM[:, sm_col + c: sm_col + c + 1],
                             in_=ap[c * 128:(c + 1) * 128].rearrange("(p o) -> p o", o=1)),
                "gsem", 16)


class CSlots:

    def __init__(self, ctx, nslots, base=0, width=512):
        self.ctx = ctx
        self.n = [0] * nslots
        self.base = base
        self.width = width

    def get(self, i, eng):
        if self.ctx.cnt[f"sl{i}"] > 0:
            eng.wait_ge(getattr(self.ctx, f"sl{i}"), self.ctx.cnt[f"sl{i}"])
        return self.ctx.C[:, self.base + i * self.width: self.base + (i + 1) * self.width]

    def stored(self, i, instr):
        self.ctx.inc(instr, f"sl{i}", 16)


def st_qkv0(ctx):
    te, ve, sc, gp = ctx.te, ctx.ve, ctx.sc, ctx.gp
    A = ctx.A
    xn = ctx.hbm["xn0"]; q0 = ctx.hbm["q0"]; kv0 = ctx.hbm["kv0"]
    wq = ctx.hbm["L0_attn_Wq"]; wkv = ctx.hbm["L0_attn_Wkv"]
    wait_all(ctx, [te, ve, sc, gp, ctx.sy])
    pool = PsumPool(ctx, [ctx.P1[:, :], ctx.P2[:, :], ctx.P3[:, :], ctx.P4[:, :]])
    slots = CSlots(ctx, 8)
    for b in range(BPC):
        for c in range(NCH):
            ctx.inc(gp.dma_start(out=A[:, c * 2048:(c + 1) * 2048],
                                 in_=xn[b, c * 128:(c + 1) * 128, :]), "gsem", 16)
        gfence(ctx)
        ld = ctx.cnt["gsem"]
        te.wait_ge(ctx.gsem, ld)

        def rhs_fn(k, tt):
            return A[:, k * 2048 + tt * 512: k * 2048 + tt * 512 + 512]

        subs = [
            (wq, q0, 0, 0.125),
            (wkv, kv0, 0, 1.0),
            (wkv, kv0, 1024, 1.0),
        ]
        for si, (wap, dst, rbase, scale) in enumerate(subs):
            fet = {}

            def lhsT_fn(k, of, wap=wap, si=si, fet=fet):
                if k not in fet:
                    if si == 0:
                        src = wap[k * 128:(k + 1) * 128, :]
                    elif si == 1:
                        src = wap[k * 128:(k + 1) * 128, 0:1024]
                    else:
                        src = wap[k * 128:(k + 1) * 128, 1024:2048]
                    fet[k] = ctx.ws.fetch(src, 1024)
                ap, wv = fet[k]
                return ap[:, of * 128:(of + 1) * 128], wv

            def evac_fn(ps, of, tt, mtgt, dst=dst, rbase=rbase, scale=scale):
                sc.wait_ge(ctx.mmS, mtgt)
                o = slots.get(of % 8, sc)
                i = sc.activation(o, ps, AF.Copy, scale=scale)
                stgt = ctx.inc(i, "scS", 1)
                gp.wait_ge(ctx.scS, stgt)
                slots.stored(of % 8, gp.dma_start(
                    out=dst[b, rbase + of * 128: rbase + (of + 1) * 128,
                            tt * 512:(tt + 1) * 512], in_=o))
                return ("scS", stgt)

            mm_proj(ctx, nk=NCH, lhsT_fn=lhsT_fn, rhs_fn=rhs_fn, evac_fn=evac_fn,
                    of_list=list(range(8)), tt_list=list(range(4)), pool=pool)
            ctx.ws.rel(te.drain(), NCH)
        gp_wait_all(ctx)
        gp.sem_inc(ctx.x1, 1); ctx.cnt["x1"] += 1
        te.wait_ge(ctx.x1, ctx.cnt["x1"])
    ctx.end_stage([te, ve, sc, gp])


def st_attn0(ctx):
    te, ve, sc, gp = ctx.te, ctx.ve, ctx.sc, ctx.gp
    A, SM = ctx.A, ctx.SM
    P0, P1, P2, P3 = ctx.P0, ctx.P1, ctx.P2, ctx.P3
    q0 = ctx.hbm["q0"]; kv0 = ctx.hbm["kv0"]; ao0 = ctx.hbm["ao0"]
    band = ctx.hbm["band16"]
    wait_all(ctx, [te, ve, sc, gp])
    QK, KK, V, VT, BD, PP, PT = 0, 2048, 4096, 6144, 7168, 11136, 13184
    AO = 0
    ctx.inc(gp.dma_start(out=SM[:, 0:128], in_=ctx.hbm["idn"][:, :]), "gsem", 16)
    ctx.inc(gp.dma_start(out=SM[:, 128:256], in_=ctx.hbm["rev"][:, :]), "gsem", 16)
    ID, RV = SM[:, 0:128], SM[:, 128:256]
    gfence(ctx)
    te.wait_ge(ctx.gsem, ctx.cnt["gsem"])
    ptfree = {0: None, 1: None}
    p3free = None
    for h in range(HEADS):
        for b in range(BPC):
            ctx.inc(gp.dma_start(out=A[0:64, QK:QK + 2048],
                                 in_=q0[b, h * 64:(h + 1) * 64, :]), "gsem", 16)
            ctx.inc(gp.dma_start(out=A[0:64, KK:KK + 2048],
                                 in_=kv0[b, h * 64:(h + 1) * 64, :]), "gsem", 16)
            ctx.inc(gp.dma_start(out=A[0:64, V:V + 2048],
                                 in_=kv0[b, 1024 + h * 64: 1024 + (h + 1) * 64, :]), "gsem", 16)
            if b == 0:
                ctx.inc(gp.dma_start(
                    out=A[:, BD:BD + 3968],
                    in_=bass.AP(band.tensor, h * 4095, [[1, 128], [1, 3968]])), "gsem", 16)
            gfence(ctx)
            ld = ctx.cnt["gsem"]
            te.wait_ge(ctx.gsem, ld)
            for g in range(4):
                if ptfree[1] is not None:
                    te.wait_ge(getattr(ctx, ptfree[1][0]), ptfree[1][1])
                i = None
                for j4 in range(4):
                    j = g * 4 + j4
                    i = te.transpose(P2[:, j4 * 128:j4 * 128 + 64],
                                     A[0:64, V + j * 128: V + (j + 1) * 128], ID[0:64, 0:64])
                mtgt = ctx.inc(i, "mmS", 1)
                ve.wait_ge(ctx.mmS, mtgt)
                i = None
                for j4 in range(4):
                    j = g * 4 + j4
                    i = ve.tensor_copy(A[:, VT + j * 64: VT + (j + 1) * 64],
                                       P2[:, j4 * 128: j4 * 128 + 64])
                ptfree[1] = ("vS", ctx.inc(i, "vS", 1))
            for m in range(16):
                for kt in range(4):
                    te.matmul(P0[:, kt * 512:(kt + 1) * 512],
                              A[0:64, QK + m * 128: QK + (m + 1) * 128],
                              A[0:64, KK + kt * 512: KK + (kt + 1) * 512],
                              start=True, stop=False)
                off = 1920 - 128 * m
                last = None
                for kt in range(4):
                    last = te.matmul(P0[:, kt * 512:(kt + 1) * 512], RV,
                                     A[:, BD + off + kt * 512: BD + off + (kt + 1) * 512],
                                     start=False, stop=True)
                mtgt = ctx.inc(last, "mmS", 1)
                ve.wait_ge(ctx.mmS, mtgt)
                i = ve.reduce_max(SM[:, 256:257], P0[:, 0:2048], axis=AX.X, negate=True)
                vtgt = ctx.inc(i, "vS", 1)
                sc.wait_ge(ctx.vS, vtgt)
                i = sc.activation(A[:, PP:PP + 2048], P0[:, 0:2048], AF.Exp,
                                  bias=SM[:, 256:257], accum_out=SM[:, 257:258])
                stgt = ctx.inc(i, "scS", 1)
                ve.wait_ge(ctx.scS, stgt)
                ve.reciprocal(SM[:, 258:259], SM[:, 257:258])
                ve.drain()
                i = ve.tensor_scalar(A[:, PP:PP + 2048], A[:, PP:PP + 2048],
                                     SM[:, 258:259], None, op0=ALU.mult)
                vtgt = ctx.inc(i, "vS", 1)
                te.wait_ge(ctx.vS, vtgt)
                ready = []
                for g in range(4):
                    bankid = g % 2
                    pg = P1 if bankid == 0 else P2
                    if ptfree[bankid] is not None:
                        te.wait_ge(getattr(ctx, ptfree[bankid][0]), ptfree[bankid][1])
                    i = None
                    for j4 in range(4):
                        j = g * 4 + j4
                        i = te.transpose(pg[:, j4 * 128:(j4 + 1) * 128],
                                         A[:, PP + j * 128: PP + (j + 1) * 128], ID)
                    mtgt = ctx.inc(i, "mmS", 1)
                    if bankid == 0:
                        ve.wait_ge(ctx.mmS, mtgt)
                        i = ve.tensor_copy(A[:, PT + g * 512: PT + (g + 1) * 512], pg[:, 0:512])
                        ptfree[0] = ("vS", ctx.inc(i, "vS", 1))
                        ready.append(ptfree[0])
                    else:
                        sc.wait_ge(ctx.mmS, mtgt)
                        i = sc.copy(A[:, PT + g * 512: PT + (g + 1) * 512], pg[:, 0:512])
                        ptfree[1] = ("scS", ctx.inc(i, "scS", 1))
                        ready.append(ptfree[1])
                for semn, tgt in ready:
                    te.wait_ge(getattr(ctx, semn), tgt)
                if p3free is not None:
                    te.wait_ge(getattr(ctx, p3free[0]), p3free[1])
                last = None
                for j in range(16):
                    last = te.matmul(P3[0:64, 0:128],
                                     A[:, VT + j * 64: VT + (j + 1) * 64],
                                     A[:, PT + j * 128: PT + (j + 1) * 128],
                                     start=(j == 0), stop=(j == 15))
                mtgt = ctx.inc(last, "mmS", 1)
                ve.wait_ge(ctx.mmS, mtgt)
                i = ve.tensor_copy(ctx.C[0:64, AO + m * 128: AO + (m + 1) * 128], P3[0:64, 0:128])
                p3free = ("vS", ctx.inc(i, "vS", 1))
            gp.wait_ge(ctx.vS, ctx.cnt["vS"])
            ctx.inc(gp.dma_start(out=ao0[b, h * 64:(h + 1) * 64, :],
                                 in_=ctx.C[0:64, AO:AO + 2048]), "gsem", 16)
            gp_wait_all(ctx)
            gp.sem_inc(ctx.x1, 1); ctx.cnt["x1"] += 1
            te.wait_ge(ctx.x1, ctx.cnt["x1"])
    ctx.end_stage([te, ve, sc, gp])


def st_addproj(ctx, src_name, res_name, w_name, b_name, dst_name, extra_name=None):
    te, ve, sc, gp = ctx.te, ctx.ve, ctx.sc, ctx.gp
    A = ctx.A
    src = ctx.hbm[src_name]; res = ctx.hbm[res_name]; dst = ctx.hbm[dst_name]
    wap = ctx.hbm[w_name]
    extra = ctx.hbm[extra_name] if extra_name else None
    wait_all(ctx, [te, ve, sc, gp, ctx.sy])
    load_bias_chunks(ctx, b_name, NCH, 0)
    gfence(ctx)
    pool = PsumPool(ctx, [ctx.P1[:, :], ctx.P2[:, :], ctx.P3[:, :], ctx.P4[:, :]])
    slots = CSlots(ctx, 8)
    bst = ctx.cnt["gsem"]
    sc.wait_ge(ctx.gsem, bst)
    for b in range(BPC):
        fet = {}

        def lhsT_fn(k, of, fet=fet):
            if k not in fet:
                fet[k] = ctx.ws.fetch(wap[k * 128:(k + 1) * 128, :], 1024)
            ap, wv = fet[k]
            return ap[:, of * 128:(of + 1) * 128], wv

        for tt in range(4):
            for c in range(NCH):
                ctx.inc(gp.dma_start(out=A[:, c * 512:(c + 1) * 512],
                                     in_=src[b, c * 128:(c + 1) * 128, tt * 512:(tt + 1) * 512]),
                        "gsem", 16)
            for c in range(NCH):
                ctx.inc(gp.dma_start(out=A[:, 4096 + c * 512: 4096 + (c + 1) * 512],
                                     in_=res[b, c * 128:(c + 1) * 128, tt * 512:(tt + 1) * 512]),
                        "gsem", 16)
            if extra is not None:
                for c in range(NCH):
                    ctx.inc(gp.dma_start(
                        out=A[:, 8192 + c * 512: 8192 + (c + 1) * 512],
                        in_=extra[b, c * 128:(c + 1) * 128, tt * 512:(tt + 1) * 512]),
                        "gsem", 16)
            gfence(ctx)
            ld = ctx.cnt["gsem"]
            te.wait_ge(ctx.gsem, ld)
            ve.wait_ge(ctx.gsem, ld)

            def rhs_fn(k, _tt):
                return A[:, k * 512:(k + 1) * 512]

            def evac_fn(ps, of, _tt, mtgt):
                sc.wait_ge(ctx.mmS, mtgt)
                o = slots.get(of % 8, sc)
                i = sc.activation(o, ps, AF.Identity, bias=ctx.SM[:, of:of + 1])
                stgt = ctx.inc(i, "scS", 1)
                ve.wait_ge(ctx.scS, stgt)
                last = ve.tensor_tensor(o, o, A[:, 4096 + of * 512: 4096 + of * 512 + 512],
                                        op=ALU.add)
                if extra is not None:
                    last = ve.tensor_tensor(o, o, A[:, 8192 + of * 512: 8192 + of * 512 + 512],
                                            op=ALU.add)
                vtgt = ctx.inc(last, "vS", 1)
                gp.wait_ge(ctx.vS, vtgt)
                slots.stored(of % 8, gp.dma_start(
                    out=dst[b, of * 128:(of + 1) * 128, tt * 512:(tt + 1) * 512], in_=o))
                return ("scS", stgt)

            mm_proj(ctx, nk=NCH, lhsT_fn=lhsT_fn, rhs_fn=rhs_fn, evac_fn=evac_fn,
                    of_list=list(range(NCH)), tt_list=[tt], pool=pool)
            gp_wait_all(ctx)
            gp.sem_inc(ctx.x1, 1); ctx.cnt["x1"] += 1
            te.wait_ge(ctx.x1, ctx.cnt["x1"])
            ve.wait_ge(ctx.x1, ctx.cnt["x1"])
        ctx.ws.rel(te.drain(), NCH)
    ctx.end_stage([te, ve, sc, gp])


def st_ffn0a(ctx):
    te, ve, sc, gp = ctx.te, ctx.ve, ctx.sc, ctx.gp
    A, C, SM = ctx.A, ctx.C, ctx.SM
    P1, P3 = ctx.P1, ctx.P3
    xn = ctx.hbm["xn0"]; fq0 = ctx.hbm["fq0"]; fkv0 = ctx.hbm["fkv0"]
    wait_all(ctx, [te, ve, sc, gp, ctx.sy])
    ctx.inc(gp.dma_start(out=SM[:, 0:128], in_=ctx.hbm["ones"][:, :]), "gsem", 16)
    load_bias_chunks(ctx, "L0_ffn_gamma1", 8, 130)
    load_bias_chunks(ctx, "L0_ffn_bq", 8, 138)
    ctx.inc(gp.dma_start(out=SM[0:64, 146:147],
                         in_=ctx.hbm["L0_ffn_bdown"][:].rearrange("(p o) -> p o", o=1)), "gsem", 16)
    load_bias_chunks(ctx, "L0_ffn_bup", 16, 147)
    gfence(ctx)
    cst = ctx.cnt["gsem"]
    ve.wait_ge(ctx.gsem, cst)
    i = ve.tensor_scalar_mul(SM[:, 138:146], SM[:, 138:146], 0.125)
    bq_t = ctx.inc(i, "vS", 1)
    ones_col = SM[:, 0:1]
    RB = 256
    pool = PsumPool(ctx, [ctx.P2[:, :], ctx.P4[:, :]])
    slots = CSlots(ctx, 8)
    for b in range(BPC):
        for c in range(NCH):
            ctx.inc(gp.dma_start(out=A[:, c * 2048:(c + 1) * 2048],
                                 in_=xn[b, c * 128:(c + 1) * 128, :]), "gsem", 16)
        gfence(ctx)
        ld = ctx.cnt["gsem"]
        p1_free = 0
        for tt in range(4):
            ve.wait_ge(ctx.gsem, ld)
            i = None
            for c in range(NCH):
                xs = A[:, c * 2048 + tt * 512: c * 2048 + tt * 512 + 512]
                i = ve.tensor_tensor(C[:, c * 512:(c + 1) * 512], xs, xs, op=ALU.mult)
            vtgt = ctx.inc(i, "vS", 1)
            te.wait_ge(ctx.gsem, ld)
            te.wait_ge(ctx.vS, vtgt)
            if p1_free:
                te.wait_ge(ctx.evS, p1_free)
            i = None
            for c in range(NCH):
                i = te.matmul(P1[0:1, 0:512], ones_col, C[:, c * 512:(c + 1) * 512],
                              start=(c == 0), stop=(c == NCH - 1))
            mtgt = ctx.inc(i, "mmS", 1)
            ve.wait_ge(ctx.mmS, mtgt)
            i = ve.tensor_copy(C[0:1, 4096 + tt * 512: 4096 + tt * 512 + 512], P1[0:1, 0:512])
            p1_free = ctx.inc(i, "evS", 1)
        ssq = C[0:1, 4096:6144]
        rstd = SM[0:1, RB:RB + 2048]
        ve.wait_ge(ctx.evS, p1_free)
        ve.drain()
        i = ve.tensor_scalar(rstd, ssq, 1.0 / D, float(RMS_EPS), op0=ALU.mult, op1=ALU.add)
        vtgt = ctx.inc(i, "vS", 1)
        sc.wait_ge(ctx.vS, vtgt)
        i = sc.activation(rstd, rstd, AF.Sqrt)
        stgt = ctx.inc(i, "scS", 1)
        ve.wait_ge(ctx.scS, stgt)
        i = ve.reciprocal(rstd, rstd)
        rstd_t = ctx.inc(i, "vS", 1)
        h1_t = 0
        for tt in range(4):
            te.wait_ge(ctx.vS, max(rstd_t, h1_t))
            i = te.matmul(P3[:, 0:512], SM[0:1, 0:128], rstd[0:1, tt * 512:(tt + 1) * 512],
                          start=True, stop=True)
            mtgt = ctx.inc(i, "mmS", 1)
            ve.wait_ge(ctx.mmS, mtgt)
            i = None
            for c in range(NCH):
                xs = A[:, c * 2048 + tt * 512: c * 2048 + tt * 512 + 512]
                i = ve.scalar_tensor_tensor(xs, xs, SM[:, 130 + c:131 + c], P3[:, 0:512],
                                            op0=ALU.mult, op1=ALU.mult)
            h1_t = ctx.inc(i, "vS", 1)
        te.wait_ge(ctx.vS, h1_t)

        def rhs_fn(k, tt):
            return A[:, k * 2048 + tt * 512: k * 2048 + tt * 512 + 512]

        fet = {}

        def lhsT_fn(k, of, fet=fet):
            if k not in fet:
                fet[k] = ctx.ws.fetch(ctx.hbm["L0_ffn_Wq"][k * 128:(k + 1) * 128, :], 1024)
            ap, wv = fet[k]
            return ap[:, of * 128:(of + 1) * 128], wv

        def evac_q(ps, of, tt, mtgt):
            sc.wait_ge(ctx.mmS, mtgt)
            sc.wait_ge(ctx.vS, bq_t)
            o = slots.get(of % 8, sc)
            i = sc.activation(o, ps, AF.Identity, bias=SM[:, 138 + of:139 + of], scale=0.125)
            stgt = ctx.inc(i, "scS", 1)
            gp.wait_ge(ctx.scS, stgt)
            slots.stored(of % 8, gp.dma_start(
                out=fq0[b, of * 128:(of + 1) * 128, tt * 512:(tt + 1) * 512], in_=o))
            return ("scS", stgt)

        mm_proj(ctx, nk=NCH, lhsT_fn=lhsT_fn, rhs_fn=rhs_fn, evac_fn=evac_q,
                of_list=list(range(8)), tt_list=list(range(4)), pool=pool)
        ctx.ws.rel(te.drain(), NCH)
        fet2 = {}

        def lhsT_kl(k, of, fet2=fet2):
            if k not in fet2:
                fet2[k] = ctx.ws.fetch(ctx.hbm["L0_ffn_Wdown"][k * 128:(k + 1) * 128, :], 64)
            ap, wv = fet2[k]
            return ap[:, 0:64], wv

        kl_t = {}

        def evac_kl(ps, of, tt, mtgt):
            sc.wait_ge(ctx.mmS, mtgt)
            i = sc.activation(C[0:64, 4096 + tt * 512: 4096 + tt * 512 + 512],
                              ps[0:64, :], AF.Identity, bias=SM[0:64, 146:147])
            stgt = ctx.inc(i, "scS", 1)
            kl_t[tt] = stgt
            return ("scS", stgt)

        mm_proj(ctx, nk=NCH, lhsT_fn=lhsT_kl, rhs_fn=rhs_fn, evac_fn=evac_kl,
                of_list=[0], tt_list=list(range(4)), pool=pool, out_rows=64)
        ctx.ws.rel(te.drain(), NCH)
        wup_f = ctx.ws.fetch(ctx.hbm["L0_ffn_Wup"][:, :], 2048, rows=64)

        def lhsT_up(k, of, wup_f=wup_f):
            ap, wv = wup_f
            return ap[0:64, of * 128:(of + 1) * 128], wv

        def rhs_up(k, tt):
            return C[0:64, 4096 + tt * 512: 4096 + tt * 512 + 512]

        def evac_up(ps, of, tt, mtgt):
            sc.wait_ge(ctx.mmS, mtgt)
            o = slots.get(of % 8, sc)
            i = sc.activation(o, ps, AF.Identity, bias=SM[:, 147 + of:148 + of])
            stgt = ctx.inc(i, "scS", 1)
            gp.wait_ge(ctx.scS, stgt)
            slots.stored(of % 8, gp.dma_start(
                out=fkv0[b, of * 128:(of + 1) * 128, tt * 512:(tt + 1) * 512], in_=o))
            return ("scS", stgt)

        for tt in range(4):
            te.wait_ge(ctx.scS, kl_t[tt])
            mm_proj(ctx, nk=1, lhsT_fn=lhsT_up, rhs_fn=rhs_up, evac_fn=evac_up,
                    of_list=list(range(16)), tt_list=[tt], pool=pool)
        ctx.ws.rel(te.drain(), 1)
        gp_wait_all(ctx)
        gp.sem_inc(ctx.x1, 1); ctx.cnt["x1"] += 1
        te.wait_ge(ctx.x1, ctx.cnt["x1"])
        ve.wait_ge(ctx.x1, ctx.cnt["x1"])
    ctx.end_stage([te, ve, sc, gp])


def st_ffn0b(ctx):
    te, ve, sc, gp = ctx.te, ctx.ve, ctx.sc, ctx.gp
    A, Bb, C, SM = ctx.A, ctx.Bb, ctx.C, ctx.SM
    P1, P2, P3 = ctx.P1, ctx.P2, ctx.P3
    fq0 = ctx.hbm["fq0"]; fkv0 = ctx.hbm["fkv0"]; ffo0 = ctx.hbm["ffo0"]
    wait_all(ctx, [te, ve, sc, gp])
    ctx.inc(gp.dma_start(out=SM[:, 0:128], in_=ctx.hbm["idn"][:, :]), "gsem", 16)
    ID = SM[:, 0:128]
    gfence(ctx)
    te.wait_ge(ctx.gsem, ctx.cnt["gsem"])
    QT, KVT, AT, OT, OC, PR = 0, 1024, 3072, 3328, 4352, 5376
    pfree = {0: 0, 1: 0}
    store_t = 0
    for b in range(BPC):
        for c in range(NCH):
            ctx.inc(gp.dma_start(out=A[:, c * 2048:(c + 1) * 2048],
                                 in_=fq0[b, c * 128:(c + 1) * 128, :]), "gsem", 16)
        gfence(ctx)
        qld = ctx.cnt["gsem"]
        for t in range(16):
            t0 = t * 128
            for j in range(16):
                ctx.inc(gp.dma_start(out=Bb[:, j * 128:(j + 1) * 128],
                                     in_=fkv0[b, j * 128:(j + 1) * 128, t0:t0 + 128]),
                        "gsem", 16)
            gfence(ctx)
            kld = ctx.cnt["gsem"]
            te.wait_ge(ctx.gsem, kld)
            srcs = [A[:, c * 2048 + t0: c * 2048 + t0 + 128] for c in range(NCH)] + \
                   [Bb[:, j * 128:(j + 1) * 128] for j in range(16)]
            dsts = [C[:, QT + c * 128: QT + (c + 1) * 128] for c in range(NCH)] + \
                   [C[:, KVT + j * 128: KVT + (j + 1) * 128] for j in range(16)]
            for g in range(6):
                bank = g % 2
                pg = P1 if bank == 0 else P2
                if pfree[bank]:
                    te.wait_ge(ctx.vS, pfree[bank])
                i = None
                for j4 in range(4):
                    i = te.transpose(pg[:, j4 * 128:(j4 + 1) * 128], srcs[g * 4 + j4], ID)
                mtgt = ctx.inc(i, "mmS", 1)
                ve.wait_ge(ctx.mmS, mtgt)
                i = None
                for j4 in range(4):
                    i = ve.tensor_copy(dsts[g * 4 + j4], pg[:, j4 * 128:(j4 + 1) * 128])
                pfree[bank] = ctx.inc(i, "vS", 1)
            ve.wait_ge(ctx.vS, pfree[0])
            ve.wait_ge(ctx.vS, pfree[1])
            attv = C[:, AT:AT + 256].rearrange("p (h g) -> p h g", g=16)
            Ct = C.ap().tensor if hasattr(C, "ap") else C.tensor
            k_op = bass.AP(Ct, KVT, [[8192, 128], [0, 2], [128, 16], [1, 64]])
            pr2 = C[:, PR:PR + 2048].rearrange("p (h g d) -> p h g d", g=16, d=64)
            for hp in range(8):
                qh = bass.AP(Ct, QT + hp * 128,
                             [[8192, 128], [64, 2], [0, 16], [1, 64]])
                ve.tensor_tensor(pr2, qh, k_op, op=ALU.mult)
                ve.drain()
                ve.tensor_reduce(attv[:, hp * 2:(hp + 1) * 2, :].unsqueeze(3),
                                 pr2, axis=AX.X, op=ALU.add)
                ve.drain()
            ve.drain()
            mx = SM[:, 256:272].rearrange("p (h o) -> p h o", o=1)
            sm = SM[:, 272:288].rearrange("p (h o) -> p h o", o=1)
            ve.tensor_reduce(mx, attv, axis=AX.X, op=ALU.max)
            ve.drain()
            i = ve.tensor_tensor(attv, attv, mx.broadcast_to([128, 16, 16]), op=ALU.subtract)
            vtgt = ctx.inc(i, "vS", 1)
            sc.wait_ge(ctx.vS, vtgt)
            i = sc.activation(C[:, AT:AT + 256], C[:, AT:AT + 256], AF.Exp)
            stgt = ctx.inc(i, "scS", 1)
            ve.wait_ge(ctx.scS, stgt)
            ve.tensor_reduce(sm, attv, axis=AX.X, op=ALU.add)
            ve.drain()
            ve.reciprocal(SM[:, 272:288], SM[:, 272:288])
            ve.drain()
            ve.tensor_tensor(attv, attv, sm.broadcast_to([128, 16, 16]), op=ALU.mult)
            ve.drain()
            ov = C[:, OT:OT + 1024].rearrange("p (h d) -> p h d", d=64)
            v_op = bass.AP(Ct, KVT + 64, [[8192, 128], [0, 2], [1, 64], [128, 16]])
            pr2o = C[:, PR:PR + 2048].rearrange("p (h d g) -> p h d g", d=64, g=16)
            i = None
            for hp in range(8):
                ah = bass.AP(Ct, AT + hp * 32,
                             [[8192, 128], [16, 2], [0, 64], [1, 16]])
                ve.tensor_tensor(pr2o, ah, v_op, op=ALU.mult)
                ve.drain()
                i = ve.tensor_reduce(ov[:, hp * 2:(hp + 1) * 2, :].unsqueeze(3),
                                     pr2o, axis=AX.X, op=ALU.add)
                ve.drain()
            ot_t = ctx.inc(i, "vS", 1)
            te.wait_ge(ctx.vS, ot_t)
            for g in range(2):
                bank = g % 2
                pg = P1 if bank == 0 else P2
                if pfree[bank]:
                    te.wait_ge(ctx.vS, pfree[bank])
                i = None
                for j4 in range(4):
                    c = g * 4 + j4
                    i = te.transpose(pg[:, j4 * 128:(j4 + 1) * 128],
                                     C[:, OT + c * 128: OT + (c + 1) * 128], ID)
                mtgt = ctx.inc(i, "mmS", 1)
                ve.wait_ge(ctx.mmS, mtgt)
                if store_t:
                    ve.wait_ge(ctx.gsem, store_t)
                i = None
                for j4 in range(4):
                    c = g * 4 + j4
                    i = ve.tensor_copy(C[:, OC + c * 128: OC + (c + 1) * 128],
                                       pg[:, j4 * 128:(j4 + 1) * 128])
                pfree[bank] = ctx.inc(i, "vS", 1)
            gp.wait_ge(ctx.vS, pfree[1])
            for c in range(NCH):
                ctx.inc(gp.dma_start(out=ffo0[b, c * 128:(c + 1) * 128, t0:t0 + 128],
                                     in_=C[:, OC + c * 128: OC + (c + 1) * 128]), "gsem", 16)
            store_t = ctx.cnt["gsem"]
            gp_wait_all(ctx)
            gp.sem_inc(ctx.x1, 1); ctx.cnt["x1"] += 1
            te.wait_ge(ctx.x1, ctx.cnt["x1"])
            ve.wait_ge(ctx.x1, ctx.cnt["x1"])
    ctx.end_stage([te, ve, sc, gp])


def st_ffn0c(ctx):
    te, ve, sc, gp = ctx.te, ctx.ve, ctx.sc, ctx.gp
    A, Bb, C, SM = ctx.A, ctx.Bb, ctx.C, ctx.SM
    P0, P1, P2, P3, P4 = ctx.P0, ctx.P1, ctx.P2, ctx.P3, ctx.P4
    ffo = ctx.hbm["ffo0"]; xn = ctx.hbm["xn0"]; r0 = ctx.hbm["r0"]; x1 = ctx.hbm["x1"]
    wait_all(ctx, [te, ve, sc, gp, ctx.sy])
    ctx.inc(gp.dma_start(out=SM[:, 0:128], in_=ctx.hbm["ones"][:, :]), "gsem", 16)
    ctx.inc(gp.dma_start(out=SM[:, 128:256], in_=ctx.hbm["idn"][:, :]), "gsem", 16)
    load_bias_chunks(ctx, "L0_ffn_gamma2", 8, 260)
    load_bias_chunks(ctx, "L0_ffn_bo", 8, 268)
    ctx.inc(gp.dma_start(out=SM[0:8, 276:277],
                         in_=ctx.hbm["L0_ffn_bg"][:].rearrange("(p o) -> p o", o=1)), "gsem", 16)
    gfence(ctx)
    cst = ctx.cnt["gsem"]
    RB, GR = 280, 2328
    pool = PsumPool(ctx, [P1[:, :], P2[:, :], P3[:, :], P4[:, :]])
    SMt = SM.ap().tensor if hasattr(SM, "ap") else SM.tensor
    for b in range(BPC):
        fet = {}

        def lhsT_wo(k, of, fet=fet):
            if k not in fet:
                fet[k] = ctx.ws.fetch(ctx.hbm["L0_ffn_Wo"][k * 128:(k + 1) * 128, :], 1024)
            ap, wv = fet[k]
            return ap[:, of * 128:(of + 1) * 128], wv

        for tt in range(4):
            for c in range(NCH):
                ctx.inc(gp.dma_start(out=Bb[:, c * 512:(c + 1) * 512],
                                     in_=ffo[b, c * 128:(c + 1) * 128, tt * 512:(tt + 1) * 512]),
                        "gsem", 16)
            for c in range(NCH):
                ctx.inc(gp.dma_start(out=Bb[:, 4096 + c * 512: 4096 + (c + 1) * 512],
                                     in_=xn[b, c * 128:(c + 1) * 128, tt * 512:(tt + 1) * 512]),
                        "gsem", 16)
            gfence(ctx)
            ld = ctx.cnt["gsem"]
            te.wait_ge(ctx.gsem, ld)
            sc.wait_ge(ctx.gsem, cst)

            def rhs_fn(k, _tt):
                return Bb[:, k * 512:(k + 1) * 512]

            def evac_fn(ps, of, _tt, mtgt, tt=tt):
                dst = A[:, of * 2048 + tt * 512: of * 2048 + tt * 512 + 512]
                sc.wait_ge(ctx.mmS, mtgt)
                i = sc.activation(dst, ps, AF.Identity, bias=SM[:, 268 + of:269 + of])
                stgt = ctx.inc(i, "scS", 1)
                ve.wait_ge(ctx.scS, stgt)
                i = ve.tensor_tensor(dst, dst, Bb[:, 4096 + of * 512: 4096 + of * 512 + 512],
                                     op=ALU.add)
                ctx.inc(i, "vS", 1)
                return ("scS", stgt)

            mm_proj(ctx, nk=NCH, lhsT_fn=lhsT_wo, rhs_fn=rhs_fn, evac_fn=evac_fn,
                    of_list=list(range(NCH)), tt_list=[tt], pool=pool)
            gp.wait_ge(ctx.vS, ctx.cnt["vS"])
        ctx.ws.rel_seq(te, NCH)
        v2_t = ctx.cnt["vS"]
        p1_free = 0
        for tt in range(4):
            ve.wait_ge(ctx.vS, v2_t)
            i = None
            for c in range(NCH):
                xs = A[:, c * 2048 + tt * 512: c * 2048 + tt * 512 + 512]
                i = ve.tensor_tensor(C[:, c * 512:(c + 1) * 512], xs, xs, op=ALU.mult)
            vtgt = ctx.inc(i, "vS", 1)
            te.wait_ge(ctx.vS, vtgt)
            if p1_free:
                te.wait_ge(ctx.evS, p1_free)
            i = None
            for c in range(NCH):
                i = te.matmul(P1[0:1, 0:512], SM[:, 0:1], C[:, c * 512:(c + 1) * 512],
                              start=(c == 0), stop=(c == NCH - 1))
            mtgt = ctx.inc(i, "mmS", 1)
            ve.wait_ge(ctx.mmS, mtgt)
            i = ve.tensor_copy(C[0:1, 4096 + tt * 512: 4096 + tt * 512 + 512], P1[0:1, 0:512])
            p1_free = ctx.inc(i, "evS", 1)
        rstd = SM[0:1, RB:RB + 2048]
        ve.wait_ge(ctx.evS, p1_free)
        ve.drain()
        i = ve.tensor_scalar(rstd, C[0:1, 4096:6144], 1.0 / D, float(RMS_EPS),
                             op0=ALU.mult, op1=ALU.add)
        vtgt = ctx.inc(i, "vS", 1)
        sc.wait_ge(ctx.vS, vtgt)
        i = sc.activation(rstd, rstd, AF.Sqrt)
        stgt = ctx.inc(i, "scS", 1)
        ve.wait_ge(ctx.scS, stgt)
        i = ve.reciprocal(rstd, rstd)
        rstd_t = ctx.inc(i, "vS", 1)
        for k in range(NCH):
            ctx.inc(gp.dma_start(out=SM[:, 6464 + k * 8: 6464 + (k + 1) * 8],
                                 in_=ctx.hbm["L0_ffn_Wg"][k * 128:(k + 1) * 128, :]),
                    "gsem", 16)
        gfence(ctx)
        wg_ld = ctx.cnt["gsem"]
        last_fin = 0
        x1_fence = 0
        for tt in range(4):
            te.wait_ge(ctx.vS, max(rstd_t, last_fin))
            i = te.matmul(P3[:, 0:512], SM[0:1, 0:128], rstd[0:1, tt * 512:(tt + 1) * 512],
                          start=True, stop=True)
            mtgt = ctx.inc(i, "mmS", 1)
            ve.wait_ge(ctx.mmS, mtgt)
            if x1_fence:
                ve.wait_ge(ctx.gsem, x1_fence)
            i = None
            for c in range(NCH):
                i = ve.scalar_tensor_tensor(
                    Bb[:, c * 512:(c + 1) * 512],
                    A[:, c * 2048 + tt * 512: c * 2048 + tt * 512 + 512],
                    SM[:, 260 + c:261 + c], P3[:, 0:512], op0=ALU.mult, op1=ALU.mult)
            h2_t = ctx.inc(i, "vS", 1)
            te.wait_ge(ctx.vS, h2_t)
            te.wait_ge(ctx.gsem, wg_ld)
            bi, ps = pool.acquire(te)
            i = None
            for k in range(NCH):
                i = te.matmul(ps[0:8, :], SM[:, 6464 + k * 8: 6464 + (k + 1) * 8],
                              Bb[:, k * 512:(k + 1) * 512],
                              start=(k == 0), stop=(k == NCH - 1))
            mtgt = ctx.inc(i, "mmS", 1)
            sc.wait_ge(ctx.mmS, mtgt)
            i = sc.activation(C[0:8, 5632:6144], ps[0:8, :], AF.Identity,
                              bias=SM[0:8, 276:277])
            glog_t = ctx.inc(i, "scS", 1)
            pool.release(bi, "scS", glog_t)
            for q in range(4):
                te.wait_ge(ctx.scS, glog_t)
                te.wait_ge(ctx.evS, ctx.cnt["evS"])
                bi2, ps2 = pool.acquire(te)
                i = te.transpose(ps2[:, 0:8], C[0:8, 5632 + q * 128: 5632 + (q + 1) * 128],
                                 SM[0:8, 128:136])
                mtgt = ctx.inc(i, "mmS", 1)
                ve.wait_ge(ctx.mmS, mtgt)
                i = ve.tensor_copy(C[:, 6144:6152], ps2[:, 0:8])
                lt_t = ctx.inc(i, "vS", 1)
                pool.release(bi2, "vS", lt_t)
                ve.drain()
                i = ve.reduce_max(C[:, 6152:6153], C[:, 6144:6152], axis=AX.X, negate=True)
                vtgt = ctx.inc(i, "vS", 1)
                sc.wait_ge(ctx.vS, vtgt)
                i = sc.activation(C[:, 6144:6152], C[:, 6144:6152], AF.Exp,
                                  bias=C[:, 6152:6153], accum_out=C[:, 6153:6154])
                stgt = ctx.inc(i, "scS", 1)
                ve.wait_ge(ctx.scS, stgt)
                ve.reciprocal(C[:, 6154:6155], C[:, 6153:6154])
                ve.drain()
                i = ve.tensor_scalar(C[:, 6144:6152], C[:, 6144:6152], C[:, 6154:6155],
                                     None, op0=ALU.mult)
                gn_t = ctx.inc(i, "vS", 1)
                te.wait_ge(ctx.vS, gn_t)
                for half in range(2):
                    bi3, ps3 = pool.acquire(te)
                    i = None
                    for e4 in range(4):
                        e = half * 4 + e4
                        i = te.transpose(ps3[0:1, e4 * 128:(e4 + 1) * 128],
                                         C[:, 6144 + e:6145 + e], SM[:, 128:256])
                    mtgt = ctx.inc(i, "mmS", 1)
                    ve.wait_ge(ctx.mmS, mtgt)
                    i2 = ve.tensor_copy(
                        bass.AP(SMt, GR + half * 2048 + q * 128,
                                [[6656, 1], [512, 4], [1, 128]]),
                        ps3[0:1, 0:512].rearrange("p (e t) -> p e t", t=128))
                    gr_t = ctx.inc(i2, "vS", 1)
                    pool.release(bi3, "vS", gr_t)
            for e in range(8):
                te.wait_ge(ctx.vS, ctx.cnt["vS"])
                bi4, ps4 = pool.acquire(te)
                i = te.matmul(ps4[:, 0:512], SM[0:1, 0:128],
                              SM[0:1, GR + e * 512: GR + (e + 1) * 512],
                              start=True, stop=True)
                mtgt = ctx.inc(i, "mmS", 1)
                ve.wait_ge(ctx.mmS, mtgt)
                i = ve.tensor_copy(C[:, e * 512:(e + 1) * 512], ps4[:, 0:512])
                gb_t = ctx.inc(i, "vS", 1)
                pool.release(bi4, "vS", gb_t)
            gb_all_t = ctx.cnt["vS"]
            for c in range(NCH):
                ctx.inc(gp.dma_start(out=Bb[:, 4096 + c * 512: 4096 + (c + 1) * 512],
                                     in_=r0[b, c * 128:(c + 1) * 128, tt * 512:(tt + 1) * 512]),
                        "gsem", 16)
            gfence(ctx)
            r0_t = ctx.cnt["gsem"]
            accs = [P0[:, 0:512], P0[:, 512:1024], P0[:, 1024:1536], P0[:, 1536:2048],
                    P1[:, :], P2[:, :], P3[:, :], P4[:, :]]
            rhs_slots = [C[:, 4608:5120], C[:, 5120:5632]]
            ridx = 0
            last_mm = None
            waited = {}
            ve.wait_ge(ctx.vS, gb_all_t)
            for e in range(NEXP):
                wef = {}
                for k in range(NCH):
                    wef[k] = ctx.ws.fetch(
                        ctx.hbm["L0_ffn_We"][e, k * 128:(k + 1) * 128, :], 1024)
                for k in range(NCH):
                    if ridx >= 2:
                        ve.wait_ge(ctx.mmS, rhs_done[ridx - 2])
                    i = ve.tensor_tensor(rhs_slots[ridx % 2], Bb[:, k * 512:(k + 1) * 512],
                                         C[:, e * 512:(e + 1) * 512], op=ALU.mult)
                    rtgt = ctx.inc(i, "vS", 1)
                    te.wait_ge(ctx.vS, rtgt)
                    ap, wv = wef[k]
                    if waited.get(wv[0], 0) < wv[1]:
                        te.wait_ge(getattr(ctx, f"ws{wv[0]}"), wv[1])
                        waited[wv[0]] = wv[1]
                    for of in range(NCH):
                        last_mm = te.matmul(accs[of], ap[:, of * 128:(of + 1) * 128],
                                            rhs_slots[ridx % 2],
                                            start=(e == 0 and k == 0),
                                            stop=(e == NEXP - 1 and k == NCH - 1))
                    t = ctx.inc(last_mm, "mmS", 1)
                    if ridx == 0:
                        rhs_done = {}
                    rhs_done[ridx] = t
                    ridx += 1
                ctx.ws.rel_seq(te, NCH)
            acc_t = ctx.cnt["mmS"]
            ve.wait_ge(ctx.gsem, r0_t)
            ve.wait_ge(ctx.vS, ctx.cnt["vS"])
            for of in range(NCH):
                ve.tensor_tensor(C[:, of * 512:(of + 1) * 512],
                                 A[:, of * 2048 + tt * 512: of * 2048 + tt * 512 + 512],
                                 Bb[:, 4096 + of * 512: 4096 + (of + 1) * 512], op=ALU.add)
            ve.wait_ge(ctx.mmS, acc_t)
            ve.drain()
            i = None
            for of in range(NCH):
                i = ve.tensor_tensor(Bb[:, of * 512:(of + 1) * 512],
                                     C[:, of * 512:(of + 1) * 512], accs[of], op=ALU.add)
            fin_t = ctx.inc(i, "vS", 1)
            last_fin = fin_t
            gp.wait_ge(ctx.vS, fin_t)
            for c in range(NCH):
                ctx.inc(gp.dma_start(out=x1[b, c * 128:(c + 1) * 128, tt * 512:(tt + 1) * 512],
                                     in_=Bb[:, c * 512:(c + 1) * 512]), "gsem", 16)
            gfence(ctx)
            x1_fence = ctx.cnt["gsem"]
    ctx.end_stage([te, ve, sc, gp])


def st_fin(ctx):
    te, ve, sc, gp = ctx.te, ctx.ve, ctx.sc, ctx.gp
    A, Bb, C = ctx.A, ctx.Bb, ctx.C
    x1 = ctx.hbm["x1"]; at1 = ctx.hbm["at1"]; ffn1 = ctx.hbm["ffn1"]
    out = ctx.hbm["out"]
    wait_all(ctx, [ve, gp])
    st_t = 0
    for b in range(BPC):
        for tt in range(4):
            for c in range(NCH):
                ctx.inc(gp.dma_start(out=A[:, c * 512:(c + 1) * 512],
                                     in_=x1[b, c * 128:(c + 1) * 128, tt * 512:(tt + 1) * 512]),
                        "gsem", 16)
            for c in range(NCH):
                ctx.inc(gp.dma_start(out=A[:, 4096 + c * 512: 4096 + (c + 1) * 512],
                                     in_=at1[b, c * 128:(c + 1) * 128, tt * 512:(tt + 1) * 512]),
                        "gsem", 16)
            for c in range(NCH):
                ctx.inc(gp.dma_start(out=A[:, 8192 + c * 512: 8192 + (c + 1) * 512],
                                     in_=ffn1[b, c * 128:(c + 1) * 128, tt * 512:(tt + 1) * 512]),
                        "gsem", 16)
            gfence(ctx)
            ld = ctx.cnt["gsem"]
            ve.wait_ge(ctx.gsem, ld)
            if st_t:
                ve.wait_ge(ctx.gsem, st_t)
            for c in range(NCH):
                ve.tensor_tensor(C[:, c * 512:(c + 1) * 512], A[:, c * 512:(c + 1) * 512],
                                 A[:, 4096 + c * 512: 4096 + (c + 1) * 512], op=ALU.add)
            ve.drain()
            i = None
            for c in range(NCH):
                i = ve.tensor_tensor(C[:, c * 512:(c + 1) * 512], C[:, c * 512:(c + 1) * 512],
                                     A[:, 8192 + c * 512: 8192 + (c + 1) * 512], op=ALU.add)
            vtgt = ctx.inc(i, "vS", 1)
            gp.wait_ge(ctx.vS, vtgt)
            for c in range(NCH):
                ctx.inc(gp.dma_start(out=out[b, c * 128:(c + 1) * 128, tt * 512:(tt + 1) * 512],
                                     in_=C[:, c * 512:(c + 1) * 512]), "gsem", 16)
            gfence(ctx)
            st_t = ctx.cnt["gsem"]
    ctx.end_stage([ve, gp])


def st_scan1(ctx):
    te, ve, sc, gp = ctx.te, ctx.ve, ctx.sc, ctx.gp
    A, Bb, C, SM = ctx.A, ctx.Bb, ctx.C, ctx.SM
    P0, P1, P2, P3, P4 = ctx.P0, ctx.P1, ctx.P2, ctx.P3, ctx.P4
    xn1 = ctx.hbm["xn1"]; outs1 = ctx.hbm["outs1"]
    wait_all(ctx, [te, ve, sc, gp, ctx.sy])
    ctx.inc(gp.dma_start(out=SM[:, 0:128], in_=ctx.hbm["ones"][:, :]), "gsem", 16)
    ctx.inc(gp.dma_start(out=SM[:, 4864:4992], in_=ctx.hbm["idn"][:, :]), "gsem", 16)
    for nm, col in [("L1_attn_eta_w", 130), ("L1_attn_alpha_w", 138), ("L1_attn_gate_w", 146)]:
        load_bias_chunks(ctx, nm, 8, col)
    for nm, col in [("L1_attn_eta_b", 154), ("L1_attn_alpha_b", 155), ("L1_attn_gate_b", 156)]:
        ctx.inc(gp.dma_start(out=SM[0:1, col:col + 1],
                             in_=ctx.hbm[nm][:].rearrange("(p o) -> p o", o=1)), "gsem", 16)
    for b in range(BPC):
        for i_ in range(NCH):
            ctx.inc(gp.dma_start(out=A[:, b * 8192 + i_ * 1024: b * 8192 + (i_ + 1) * 1024],
                                 in_=ctx.hbm["M_initT"][i_ * 128:(i_ + 1) * 128, :]), "gsem", 16)
    gfence(ctx)
    cst = ctx.cnt["gsem"]
    te.wait_ge(ctx.gsem, cst)
    ve.wait_ge(ctx.gsem, cst)
    sc.wait_ge(ctx.gsem, cst)
    ones_col = SM[:, 0:1]

    def W(b):
        return 256 + b * 1536
    SR0, SR1, SR2 = 3328, 3840, 4352

    for b in range(BPC):
        Wb = W(b)
        ve.memset(SM[0:1, Wb + 160:Wb + 161], 1.0)
        i = ve.memset(SM[:, Wb + 176:Wb + 177], 1.0)
        ctx.inc(i, "vS", 1)
        for tt in range(4):
            gp.wait_ge(ctx.evS, ctx.cnt["evS"])
            gp.wait_ge(ctx.mmS, ctx.cnt["mmS"])
            for c in range(NCH):
                ctx.inc(gp.dma_start(
                    out=Bb[:, b * 4096 + c * 512: b * 4096 + (c + 1) * 512],
                    in_=xn1[b, c * 128:(c + 1) * 128, tt * 512:(tt + 1) * 512]), "gsem", 16)
            gfence(ctx)
            ld = ctx.cnt["gsem"]
            ve.wait_ge(ctx.gsem, ld)
            ve.wait_ge(ctx.evS, ctx.cnt["evS"])
            i = None
            for c in range(NCH):
                xs = Bb[:, b * 4096 + c * 512: b * 4096 + (c + 1) * 512]
                i = ve.tensor_tensor(C[:, c * 512:(c + 1) * 512], xs, xs, op=ALU.mult)
            sq_t = ctx.inc(i, "vS", 1)
            te.wait_ge(ctx.gsem, ld)
            te.wait_ge(ctx.vS, sq_t)
            te.wait_ge(ctx.evS, ctx.cnt["evS"])
            for c in range(NCH):
                te.matmul(P1[0:1, 0:512], ones_col, C[:, c * 512:(c + 1) * 512],
                          start=(c == 0), stop=(c == NCH - 1))
            for c in range(NCH):
                xs = Bb[:, b * 4096 + c * 512: b * 4096 + (c + 1) * 512]
                te.matmul(P2[0:1, 0:512], SM[:, 130 + c:131 + c], xs,
                          start=(c == 0), stop=(c == NCH - 1))
            i = None
            for c in range(NCH):
                xs = Bb[:, b * 4096 + c * 512: b * 4096 + (c + 1) * 512]
                i = te.matmul(P3[0:1, 0:512], SM[:, 138 + c:139 + c], xs,
                              start=(c == 0), stop=(c == NCH - 1))
            mtgt = ctx.inc(i, "mmS", 1)
            ve.wait_ge(ctx.mmS, mtgt)
            ve.tensor_copy(SM[0:1, SR0:SR0 + 512], P1[0:1, 0:512])
            ve.tensor_copy(SM[0:1, SR1:SR1 + 512], P2[0:1, 0:512])
            i = ve.tensor_copy(SM[0:1, SR2:SR2 + 512], P3[0:1, 0:512])
            ev_t = ctx.inc(i, "evS", 1)
            sc.wait_ge(ctx.evS, ev_t)
            i = sc.activation(SM[0:1, SR0:SR0 + 512], SM[0:1, SR0:SR0 + 512], AF.Sqrt)
            ctx.inc(i, "scS", 1)
            i = sc.activation(SM[0:1, SR1:SR1 + 512], SM[0:1, SR1:SR1 + 512], AF.Sigmoid,
                              bias=SM[0:1, 154:155])
            i2 = sc.activation(SM[0:1, SR2:SR2 + 512], SM[0:1, SR2:SR2 + 512], AF.Sigmoid,
                               bias=SM[0:1, 155:156])
            sg_t = ctx.inc(i2, "scS", 1)
            ve.wait_ge(ctx.scS, sg_t)
            ve.tensor_scalar_max(SM[0:1, SR0:SR0 + 512], SM[0:1, SR0:SR0 + 512],
                                 float(KEY_EPS))
            ve.drain()
            ve.reciprocal(SM[0:1, SR0:SR0 + 512], SM[0:1, SR0:SR0 + 512])
            ve.drain()
            i = ve.tensor_scalar_mul(SM[0:1, SR0:SR0 + 512], SM[0:1, SR0:SR0 + 512],
                                     1.0 / CHUNK)
            rn_t = ctx.inc(i, "vS", 1)
            ve.tensor_reduce(SM[0:1, Wb + tt * 8: Wb + tt * 8 + 8].unsqueeze(2),
                             SM[0:1, SR1:SR1 + 512].rearrange("p (c t) -> p c t", t=64),
                             axis=AX.X, op=ALU.add)
            ve.tensor_reduce(SM[0:1, Wb + 32 + tt * 8: Wb + 32 + tt * 8 + 8].unsqueeze(2),
                             SM[0:1, SR2:SR2 + 512].rearrange("p (c t) -> p c t", t=64),
                             axis=AX.X, op=ALU.add)
            te.wait_ge(ctx.vS, rn_t)
            i = te.matmul(P4[:, 0:512], SM[0:1, 0:128], SM[0:1, SR0:SR0 + 512],
                          start=True, stop=True)
            mtgt = ctx.inc(i, "mmS", 1)
            ve.wait_ge(ctx.mmS, mtgt)
            for c in range(NCH):
                xs = Bb[:, b * 4096 + c * 512: b * 4096 + (c + 1) * 512]
                ve.tensor_tensor(C[:, c * 512:(c + 1) * 512], xs, P4[:, 0:512], op=ALU.mult)
            ve.drain()
            i = None
            for c in range(NCH):
                base = 8192 + b * 256 + c * 32 + tt * 8
                i = ve.tensor_reduce(
                    Bb[:, base:base + 8].unsqueeze(2),
                    C[:, c * 512:(c + 1) * 512].rearrange("p (s t) -> p s t", t=64),
                    axis=AX.X, op=ALU.add)
            ctx.inc(i, "evS", 1)
        ve.drain()
        ve.tensor_scalar_mul(SM[0:1, Wb:Wb + 32], SM[0:1, Wb:Wb + 32],
                             float(MAX_LR) / CHUNK)
        i = ve.tensor_scalar(SM[0:1, Wb + 32:Wb + 64], SM[0:1, Wb + 32:Wb + 64],
                             0.5 / CHUNK, 0.5, op0=ALU.mult, op1=ALU.add)
        ea_t = ctx.inc(i, "vS", 1)
        ve.wait_ge(ctx.evS, ctx.cnt["evS"])
        te.wait_ge(ctx.evS, ctx.cnt["evS"])
        i = None
        for c in range(NCH):
            base = 8192 + b * 256 + c * 32
            i = te.matmul(P1[0:1, 0:32], SM[:, 146 + c:147 + c], Bb[:, base:base + 32],
                          start=(c == 0), stop=(c == NCH - 1))
        mtgt = ctx.inc(i, "mmS", 1)
        ve.wait_ge(ctx.mmS, mtgt)
        i = ve.tensor_copy(SM[0:1, SR0:SR0 + 32], P1[0:1, 0:32])
        vtgt = ctx.inc(i, "vS", 1)
        sc.wait_ge(ctx.vS, vtgt)
        i = sc.activation(SM[0:1, Wb + 64:Wb + 96], SM[0:1, SR0:SR0 + 32], AF.Sigmoid,
                          bias=SM[0:1, 156:157])
        g_t = ctx.inc(i, "scS", 1)
        ve.wait_ge(ctx.scS, g_t)
        ve.wait_ge(ctx.vS, ea_t)
        ve.tensor_scalar_add(SM[0:1, SR0:SR0 + 32], SM[0:1, Wb + 32:Wb + 64], -1.0)
        ve.drain()
        ve.tensor_tensor(SM[0:1, SR0:SR0 + 32], SM[0:1, SR0:SR0 + 32],
                         SM[0:1, Wb + 64:Wb + 96], op=ALU.mult)
        ve.drain()
        ve.tensor_scalar_add(SM[0:1, Wb + 96:Wb + 128], SM[0:1, SR0:SR0 + 32], 1.0)
        ve.tensor_tensor(SM[0:1, SR1:SR1 + 32], SM[0:1, Wb:Wb + 32],
                         SM[0:1, Wb + 64:Wb + 96], op=ALU.mult)
        ve.drain()
        i = ve.tensor_scalar_mul(SM[0:1, Wb + 128:Wb + 160], SM[0:1, SR1:SR1 + 32], -1.0)
        ctx.inc(i, "vS", 1)

    MAGIC = 0x5F3759DF
    mt_free = {0: 0, 1: 0}
    out_bank_free = {0: 0, 1: 0}
    bank_free = {0: 0, 1: 0}
    tb = {0: P1, 1: P2}
    xld = {0: ctx.cnt["gsem"], 1: ctx.cnt["gsem"]}
    for ci in range(N_CHUNKS):
        tt = ci // 8
        step_w = [{}, {}]
        sc_t = {}
        t1_t = {}
        t2_m = {}
        for b in range(BPC):
            Wb = W(b)
            boff = b * 8192
            if ci % 8 == 0:
                gp.wait_ge(ctx.mmS, ctx.cnt["mmS"])
                for c in range(NCH):
                    ctx.inc(gp.dma_start(
                        out=Bb[:, b * 4096 + c * 512: b * 4096 + (c + 1) * 512],
                        in_=xn1[b, c * 128:(c + 1) * 128, tt * 512:(tt + 1) * 512]),
                        "gsem", 16)
                gfence(ctx)
                xld[b] = ctx.cnt["gsem"]
            te.wait_ge(ctx.gsem, xld[b])
            if mt_free[b]:
                te.wait_ge(ctx.vS, mt_free[b])
            if out_bank_free[b]:
                te.wait_ge(ctx.vS, out_bank_free[b])
            last = None
            for of in range(NCH):
                for i_ in range(NCH):
                    last = te.matmul(
                        P0[:, b * 512 + of * 64: b * 512 + (of + 1) * 64],
                        A[:, boff + i_ * 1024 + of * 128: boff + i_ * 1024 + (of + 1) * 128],
                        Bb[:, b * 4096 + i_ * 512 + (ci % 8) * 64:
                           b * 4096 + i_ * 512 + (ci % 8) * 64 + 64],
                        start=(i_ == 0), stop=(i_ == NCH - 1))
            m_out = ctx.inc(last, "mmS", 1)
            ve.wait_ge(ctx.mmS, m_out)
            if bank_free[b]:
                ve.wait_ge(ctx.vS, bank_free[b])
            if ctx.cnt[f"sl{b}"]:
                ve.wait_ge(getattr(ctx, f"sl{b}"), ctx.cnt[f"sl{b}"])
            i = ve.tensor_scalar(C[:, b * 2048: b * 2048 + 512],
                                 P0[:, b * 512:(b + 1) * 512],
                                 SM[:, Wb + 176:Wb + 177], None, op0=ALU.mult)
            sc_t[b] = ctx.inc(i, "vS", 1)
            out_bank_free[b] = sc_t[b]
            gp.wait_ge(ctx.vS, sc_t[b])
            for of in range(NCH):
                ctx.inc(gp.dma_start(
                    out=outs1[b, of * 128:(of + 1) * 128, ci * 64:(ci + 1) * 64],
                    in_=C[:, b * 2048 + of * 64: b * 2048 + (of + 1) * 64]), f"sl{b}", 16)
        for layer in range(2):
            fdict = step_w[layer]
            wname = "vg_w1T" if layer == 0 else "vg_w2T"
            for o in range(NCH):
                for hf in range(2):
                    fdict[(o, hf)] = ctx.ws.fetch(
                        ctx.hbm[wname][o * 128:(o + 1) * 128,
                                       hf * 512:(hf + 1) * 512], 512)
            waited = {}
            for b in range(BPC):
                if layer == 0:
                    te.wait_ge(ctx.vS, max(sc_t[b], bank_free[b]))
                    rhs_off = b * 2048
                else:
                    te.wait_ge(ctx.scS, t1_t[b])
                    rhs_off = b * 2048 + 512
                last = None
                for hf in range(2):
                    for ofq in range(4):
                        of = hf * 4 + ofq
                        for o in range(NCH):
                            ap, wv = fdict[(o, hf)]
                            if waited.get(wv[0], 0) < wv[1]:
                                te.wait_ge(getattr(ctx, f"ws{wv[0]}"), wv[1])
                                waited[wv[0]] = wv[1]
                            last = te.matmul(
                                tb[b][:, of * 64:(of + 1) * 64],
                                ap[:, ofq * 128:(ofq + 1) * 128],
                                C[:, rhs_off + o * 64: rhs_off + (o + 1) * 64],
                                start=(o == 0), stop=(o == NCH - 1))
                m_t = ctx.inc(last, "mmS", 1)
                if layer == 0:
                    sc.wait_ge(ctx.mmS, m_t)
                    i = sc.activation(C[:, b * 2048 + 512: b * 2048 + 1024],
                                      tb[b][:, 0:512], AF.Silu)
                    t1_t[b] = ctx.inc(i, "scS", 1)
                else:
                    t2_m[b] = m_t
            ctx.ws.rel_seq(te, 16)
        for b in range(BPC):
            Wb = W(b)
            boff = b * 8192
            kmoff = 8192 + b * 256
            ve.wait_ge(ctx.mmS, t2_m[b])
            i = ve.tensor_tensor(C[:, b * 2048 + 1024: b * 2048 + 1536], tb[b][:, 0:512],
                                 C[:, b * 2048: b * 2048 + 512], op=ALU.add)
            vg_t = ctx.inc(i, "vS", 1)
            ve.drain()
            ve.tensor_reduce(SM[:, Wb + 180:Wb + 188].unsqueeze(2),
                             C[:, b * 2048 + 1024: b * 2048 + 1536]
                             .rearrange("p (s t) -> p s t", t=64), axis=AX.X, op=ALU.add)
            ve.drain()
            ve.tensor_scalar_mul(SM[:, Wb + 180:Wb + 188], SM[:, Wb + 180:Wb + 188],
                                 1.0 / CHUNK)
            ve.drain()
            i = ve.tensor_tensor(SM[0:1, Wb + 161:Wb + 162],
                                 SM[0:1, Wb + 96 + ci:Wb + 97 + ci],
                                 SM[0:1, Wb + 160:Wb + 161], op=ALU.mult)
            c1_t = ctx.inc(i, "vS", 1)
            te.wait_ge(ctx.vS, vg_t)
            last = None
            for of in range(NCH):
                for i_ in range(NCH):
                    last = te.matmul(
                        tb[b][:, of:of + 1],
                        A[:, boff + i_ * 1024 + of * 128: boff + i_ * 1024 + (of + 1) * 128],
                        Bb[:, kmoff + i_ * 32 + ci: kmoff + i_ * 32 + ci + 1],
                        start=(i_ == 0), stop=(i_ == NCH - 1))
            mk_t = ctx.inc(last, "mmS", 1)
            ve.wait_ge(ctx.mmS, mk_t)
            i = ve.scalar_tensor_tensor(SM[:, Wb + 188:Wb + 196], tb[b][:, 0:8],
                                        SM[:, Wb + 176:Wb + 177], SM[:, Wb + 180:Wb + 188],
                                        op0=ALU.mult, op1=ALU.subtract)
            err_t = ctx.inc(i, "vS", 1)
            te.wait_ge(ctx.vS, err_t)
            for rnd in range(2):
                last = None
                for j4 in range(4):
                    j = rnd * 4 + j4
                    last = te.transpose(tb[b][0:1, j4 * 128:(j4 + 1) * 128],
                                        SM[:, Wb + 188 + j:Wb + 189 + j], SM[:, 4864:4992])
                mtgt = ctx.inc(last, "mmS", 1)
                ve.wait_ge(ctx.mmS, mtgt)
                i = ve.tensor_scalar(
                    SM[0:1, Wb + 256 + rnd * 512: Wb + 256 + (rnd + 1) * 512],
                    tb[b][0:1, 0:512], SM[0:1, Wb + 128 + ci:Wb + 129 + ci],
                    None, op0=ALU.mult)
                er_t = ctx.inc(i, "vS", 1)
                te.wait_ge(ctx.vS, er_t)
            te.wait_ge(ctx.vS, c1_t)
            i = te.matmul(tb[b][:, 0:1], SM[0:1, 0:128], SM[0:1, Wb + 161:Wb + 162],
                          start=True, stop=True)
            mtgt = ctx.inc(i, "mmS", 1)
            ve.wait_ge(ctx.mmS, mtgt)
            i = ve.tensor_copy(SM[:, Wb + 177:Wb + 178], tb[b][:, 0:1])
            cp_t = ctx.inc(i, "vS", 1)
            te.wait_ge(ctx.vS, cp_t)
            eb_t = None
            for hf in range(2):
                i = te.matmul(tb[b][:, 0:512], SM[0:1, 0:128],
                              SM[0:1, Wb + 256 + hf * 512: Wb + 256 + (hf + 1) * 512],
                              start=True, stop=True)
                mtgt = ctx.inc(i, "mmS", 1)
                ve.wait_ge(ctx.mmS, mtgt)
                i = ve.tensor_copy(
                    C[:, 4096 + b * 1024 + hf * 512: 4096 + b * 1024 + (hf + 1) * 512],
                    tb[b][:, 0:512])
                eb_t = ctx.inc(i, "vS", 1)
                if hf == 0:
                    te.wait_ge(ctx.vS, eb_t)
            gp.wait_ge(ctx.vS, eb_t)
            up_t = None
            for i_ in range(NCH):
                gi = gp.tensor_scalar(C[:, 6144 + b * 1024: 6144 + b * 1024 + 1024],
                                      C[:, 4096 + b * 1024: 4096 + b * 1024 + 1024],
                                      Bb[:, kmoff + i_ * 32 + ci: kmoff + i_ * 32 + ci + 1],
                                      None, op0=ALU.mult)
                gu_t = ctx.inc(gi, "x1", 1)
                ve.wait_ge(ctx.x1, gu_t)
                i = ve.scalar_tensor_tensor(
                    A[:, boff + i_ * 1024: boff + (i_ + 1) * 1024],
                    A[:, boff + i_ * 1024: boff + (i_ + 1) * 1024],
                    SM[:, Wb + 177:Wb + 178],
                    C[:, 6144 + b * 1024: 6144 + b * 1024 + 1024],
                    op0=ALU.mult, op1=ALU.add)
                up_t = ctx.inc(i, "vS", 1)
                if i_ < NCH - 1:
                    gp.wait_ge(ctx.vS, up_t)
            mt_free[b] = up_t
            if ci < N_CHUNKS - 1:
                ve.drain()
                i = None
                for i_ in range(NCH):
                    i = ve.tensor_tensor_reduce(
                        out=C[:, 6144 + b * 1024: 6144 + b * 1024 + 1024],
                        in0=A[:, boff + i_ * 1024: boff + (i_ + 1) * 1024],
                        in1=A[:, boff + i_ * 1024: boff + (i_ + 1) * 1024],
                        scale=1.0, scalar=0.0, op0=ALU.mult, op1=ALU.add,
                        accum_out=SM[:, Wb + 196 + i_:Wb + 197 + i_])
                    ve.drain()
                nr_t = ctx.inc(i, "vS", 1)
                te.wait_ge(ctx.vS, nr_t)
                i = te.matmul(tb[b][0:1, 0:8], ones_col, SM[:, Wb + 196:Wb + 204],
                              start=True, stop=True)
                mtgt = ctx.inc(i, "mmS", 1)
                ve.wait_ge(ctx.mmS, mtgt)
                i = ve.tensor_copy(SM[0:1, Wb + 162:Wb + 170], tb[b][0:1, 0:8])
                ctx.inc(i, "vS", 1)
                ve.drain()
                ve.tensor_reduce(SM[0:1, Wb + 170:Wb + 171].unsqueeze(2),
                                 SM[0:1, Wb + 162:Wb + 170].unsqueeze(1),
                                 axis=AX.X, op=ALU.add)
                ve.drain()
                SS = SM[0:1, Wb + 170:Wb + 171]
                U_ = SM[0:1, Wb + 171:Wb + 172]
                T_ = SM[0:1, Wb + 172:Wb + 173]
                iv = SS.bitcast(mybir.dt.int32)
                uv = U_.bitcast(mybir.dt.int32)
                ve.tensor_scalar(uv, iv, 1, None, op0=ALU.arith_shift_right)
                ve.drain()
                ve.tensor_scalar(uv, uv, -1, MAGIC, op0=ALU.mult, op1=ALU.add)
                ve.drain()
                for _ in range(3):
                    ve.tensor_tensor(T_, U_, U_, op=ALU.mult)
                    ve.drain()
                    ve.tensor_tensor(T_, T_, SS, op=ALU.mult)
                    ve.drain()
                    ve.tensor_scalar(T_, T_, -0.5, 1.5, op0=ALU.mult, op1=ALU.add)
                    ve.drain()
                    ve.tensor_tensor(U_, U_, T_, op=ALU.mult)
                    ve.drain()
                ve.tensor_tensor(T_, SS, U_, op=ALU.mult)
                ve.drain()
                ve.tensor_scalar_add(T_, T_, 1e-6)
                ve.drain()
                ve.reciprocal(T_, T_)
                ve.drain()
                i = ve.tensor_scalar(SM[0:1, Wb + 160:Wb + 161], T_, 30.0, 1.0,
                                     op0=ALU.mult, op1=ALU.min)
                s_t = ctx.inc(i, "vS", 1)
                te.wait_ge(ctx.vS, s_t)
                i = te.matmul(tb[b][:, 0:1], SM[0:1, 0:128], SM[0:1, Wb + 160:Wb + 161],
                              start=True, stop=True)
                mtgt = ctx.inc(i, "mmS", 1)
                ve.wait_ge(ctx.mmS, mtgt)
                i = ve.tensor_copy(SM[:, Wb + 176:Wb + 177], tb[b][:, 0:1])
                bank_free[b] = ctx.inc(i, "vS", 1)
            else:
                bank_free[b] = up_t
    ctx.end_stage([te, ve, sc, gp])


def st_vg1(ctx):
    te, ve, sc, gp = ctx.te, ctx.ve, ctx.sc, ctx.gp
    Bb, C = ctx.Bb, ctx.C
    outs = ctx.hbm["outs1"]; at1 = ctx.hbm["at1"]
    wait_all(ctx, [te, ve, sc, gp, ctx.sy])
    pool = PsumPool(ctx, [ctx.P1[:, :], ctx.P2[:, :], ctx.P3[:, :], ctx.P4[:, :]])
    slots = CSlots(ctx, 8, base=4096)
    for b in range(BPC):
        for tt in range(4):
            gp.wait_ge(ctx.mmS, ctx.cnt["mmS"])
            gp.wait_ge(ctx.vS, ctx.cnt["vS"])
            for c in range(NCH):
                ctx.inc(gp.dma_start(out=Bb[:, c * 512:(c + 1) * 512],
                                     in_=outs[b, c * 128:(c + 1) * 128, tt * 512:(tt + 1) * 512]),
                        "gsem", 16)
            gfence(ctx)
            ld = ctx.cnt["gsem"]
            te.wait_ge(ctx.gsem, ld)
            ve.wait_ge(ctx.gsem, ld)
            fet = {}

            def lhsT1(k, of, fet=fet):
                if k not in fet:
                    fet[k] = ctx.ws.fetch(ctx.hbm["vg_w1T"][k * 128:(k + 1) * 128, :], 1024)
                ap, wv = fet[k]
                return ap[:, of * 128:(of + 1) * 128], wv

            def rhs1(k, _tt):
                return Bb[:, k * 512:(k + 1) * 512]

            t1_t = {}

            def evac1(ps, of, _tt, mtgt):
                sc.wait_ge(ctx.mmS, mtgt)
                i = sc.activation(C[:, of * 512:(of + 1) * 512], ps, AF.Silu)
                t1_t[of] = ctx.inc(i, "scS", 1)
                return ("scS", t1_t[of])

            mm_proj(ctx, nk=NCH, lhsT_fn=lhsT1, rhs_fn=rhs1, evac_fn=evac1,
                    of_list=list(range(NCH)), tt_list=[tt], pool=pool)
            ctx.ws.rel_seq(te, NCH)
            fet2 = {}

            def lhsT2(k, of, fet2=fet2):
                if k not in fet2:
                    fet2[k] = ctx.ws.fetch(ctx.hbm["vg_w2T"][k * 128:(k + 1) * 128, :], 1024)
                ap, wv = fet2[k]
                return ap[:, of * 128:(of + 1) * 128], wv

            def rhs2(k, _tt):
                return C[:, k * 512:(k + 1) * 512]

            def evac2(ps, of, _tt, mtgt):
                ve.wait_ge(ctx.mmS, mtgt)
                o = slots.get(of % 8, ve)
                i = ve.tensor_tensor(o, ps, Bb[:, of * 512:(of + 1) * 512], op=ALU.add)
                vtgt = ctx.inc(i, "vS", 1)
                gp.wait_ge(ctx.vS, vtgt)
                slots.stored(of % 8, gp.dma_start(
                    out=at1[b, of * 128:(of + 1) * 128, tt * 512:(tt + 1) * 512],
                    in_=o))
                return ("vS", vtgt)

            te.wait_ge(ctx.scS, ctx.cnt["scS"])
            mm_proj(ctx, nk=NCH, lhsT_fn=lhsT2, rhs_fn=rhs2, evac_fn=evac2,
                    of_list=list(range(NCH)), tt_list=[tt], pool=pool)
            ctx.ws.rel_seq(te, NCH)
    ctx.end_stage([te, ve, sc, gp])


def st_ffn1(ctx):
    te, ve, sc, gp = ctx.te, ctx.ve, ctx.sc, ctx.gp
    A, Bb, C, SM = ctx.A, ctx.Bb, ctx.C, ctx.SM
    P0, P1, P2, P3, P4 = ctx.P0, ctx.P1, ctx.P2, ctx.P3, ctx.P4
    xn1 = ctx.hbm["xn1"]; ffn1 = ctx.hbm["ffn1"]
    wait_all(ctx, [te, ve, sc, gp, ctx.sy])
    ctx.inc(gp.dma_start(out=SM[:, 0:128], in_=ctx.hbm["ones"][:, :]), "gsem", 16)
    ctx.inc(gp.dma_start(out=SM[:, 128:256], in_=ctx.hbm["idn"][:, :]), "gsem", 16)
    for k in range(NCH):
        ctx.inc(gp.dma_start(out=SM[:, 260 + k * 8: 260 + (k + 1) * 8],
                             in_=ctx.hbm["L1_ffn_Wg"][k * 128:(k + 1) * 128, :]), "gsem", 16)
    ctx.inc(gp.dma_start(out=SM[0:8, 324:325],
                         in_=ctx.hbm["L1_ffn_bg"][:].rearrange("(p o) -> p o", o=1)), "gsem", 16)
    load_bias_chunks(ctx, "L1_ffn_gamma", 8, 326)
    load_bias_chunks(ctx, "L1_ffn_beta", 8, 334)
    gfence(ctx)
    cst = ctx.cnt["gsem"]
    te.wait_ge(ctx.gsem, cst)
    sc.wait_ge(ctx.gsem, cst)
    ve.wait_ge(ctx.gsem, cst)
    ones_col = SM[:, 0:1]
    SMt = SM.ap().tensor if hasattr(SM, "ap") else SM.tensor
    GR = 2304
    pool = PsumPool(ctx, [P1[:, :], P2[:, :], P3[:, :], P4[:, :]])
    accs = [P0[:, 0:512], P0[:, 512:1024], P0[:, 1024:1536], P0[:, 1536:2048],
            P1[:, :], P2[:, :], P3[:, :], P4[:, :]]
    last_fin = 0
    st_fence = 0
    mix_t = 0
    for b in range(BPC):
        for tt in range(4):
            acc_started = False
            gp.wait_ge(ctx.mmS, ctx.cnt["mmS"])
            gp.wait_ge(ctx.vS, ctx.cnt["vS"])
            for c in range(NCH):
                ctx.inc(gp.dma_start(out=Bb[:, c * 512:(c + 1) * 512],
                                     in_=xn1[b, c * 128:(c + 1) * 128, tt * 512:(tt + 1) * 512]),
                        "gsem", 16)
            gfence(ctx)
            ld = ctx.cnt["gsem"]
            te.wait_ge(ctx.gsem, ld)
            ve.wait_ge(ctx.gsem, ld)
            if last_fin:
                te.wait_ge(ctx.vS, last_fin)
            bi, ps = pool.acquire(te)
            i = None
            for k in range(NCH):
                i = te.matmul(ps[0:8, :], SM[:, 260 + k * 8: 260 + (k + 1) * 8],
                              Bb[:, k * 512:(k + 1) * 512],
                              start=(k == 0), stop=(k == NCH - 1))
            mtgt = ctx.inc(i, "mmS", 1)
            sc.wait_ge(ctx.mmS, mtgt)
            i = sc.activation(SM[0:8, 1280:1792], ps[0:8, :], AF.Identity,
                              bias=SM[0:8, 324:325])
            glog_t = ctx.inc(i, "scS", 1)
            pool.release(bi, "scS", glog_t)
            for q in range(4):
                te.wait_ge(ctx.scS, glog_t)
                bi2, ps2 = pool.acquire(te)
                i = te.transpose(ps2[:, 0:8], SM[0:8, 1280 + q * 128: 1280 + (q + 1) * 128],
                                 SM[0:8, 128:136])
                mtgt = ctx.inc(i, "mmS", 1)
                ve.wait_ge(ctx.mmS, mtgt)
                i = ve.tensor_copy(SM[:, 1792:1800], ps2[:, 0:8])
                lt_t = ctx.inc(i, "vS", 1)
                pool.release(bi2, "vS", lt_t)
                ve.drain()
                ve.reduce_max(SM[:, 1800:1801], SM[:, 1792:1800], axis=AX.X)
                ve.drain()
                ve.tensor_scalar(SM[:, 1801:1809], SM[:, 1792:1800], SM[:, 1800:1801],
                                 None, op0=ALU.is_equal)
                ve.drain()
                ve.scalar_tensor_tensor(SM[:, 1809:1817], SM[:, 1801:1809], -1e30,
                                        SM[:, 1792:1800], op0=ALU.mult, op1=ALU.add)
                ve.drain()
                ve.reduce_max(SM[:, 1817:1818], SM[:, 1809:1817], axis=AX.X)
                ve.drain()
                i = ve.tensor_tensor(SM[:, 1818:1819], SM[:, 1800:1801],
                                     SM[:, 1817:1818], op=ALU.subtract)
                vtgt = ctx.inc(i, "vS", 1)
                sc.wait_ge(ctx.vS, vtgt)
                i = sc.activation(SM[:, 1819:1820], SM[:, 1818:1819], AF.Sigmoid)
                stgt = ctx.inc(i, "scS", 1)
                ve.wait_ge(ctx.scS, stgt)
                ve.tensor_scalar(SM[:, 1820:1821], SM[:, 1819:1820], -1.0, 1.0,
                                 op0=ALU.mult, op1=ALU.add)
                ve.drain()
                ve.tensor_tensor(SM[:, 1821:1822], SM[:, 1819:1820], SM[:, 1820:1821],
                                 op=ALU.subtract)
                ve.drain()
                ve.tensor_scalar(SM[:, 1822:1830], SM[:, 1792:1800], SM[:, 1817:1818],
                                 None, op0=ALU.is_ge)
                ve.drain()
                ve.tensor_scalar(SM[:, 1830:1838], SM[:, 1801:1809], SM[:, 1821:1822],
                                 None, op0=ALU.mult)
                ve.drain()
                i = ve.scalar_tensor_tensor(SM[:, 1830:1838], SM[:, 1822:1830],
                                            SM[:, 1820:1821], SM[:, 1830:1838],
                                            op0=ALU.mult, op1=ALU.add)
                we_t = ctx.inc(i, "vS", 1)
                te.wait_ge(ctx.vS, we_t)
                for half in range(2):
                    bi3, ps3 = pool.acquire(te)
                    i = None
                    for e4 in range(4):
                        e = half * 4 + e4
                        i = te.transpose(ps3[0:1, e4 * 128:(e4 + 1) * 128],
                                         SM[:, 1830 + e:1831 + e], SM[:, 128:256])
                    mtgt = ctx.inc(i, "mmS", 1)
                    ve.wait_ge(ctx.mmS, mtgt)
                    i2 = ve.tensor_copy(
                        bass.AP(SMt, GR + half * 2048 + q * 128,
                                [[6656, 1], [512, 4], [1, 128]]),
                        ps3[0:1, 0:512].rearrange("p (e t) -> p e t", t=128))
                    gr_t = ctx.inc(i2, "vS", 1)
                    pool.release(bi3, "vS", gr_t)
            gr_all = ctx.cnt["vS"]
            for e in range(NEXP):
                gp.wait_ge(ctx.scS, ctx.cnt["scS"])
                gp.wait_ge(ctx.vS, ctx.cnt["vS"])
                for f in range(16):
                    ctx.inc(gp.dma_start(
                        out=SM[:, 342 + f:343 + f],
                        in_=ctx.hbm["L1_ffn_b1"][e, f * 128:(f + 1) * 128]
                        .rearrange("(p o) -> p o", o=1)), "gsem", 16)
                for f in range(NCH):
                    ctx.inc(gp.dma_start(
                        out=SM[:, 358 + f:359 + f],
                        in_=ctx.hbm["L1_ffn_b2"][e, f * 128:(f + 1) * 128]
                        .rearrange("(p o) -> p o", o=1)), "gsem", 16)
                gfence(ctx)
                bld = ctx.cnt["gsem"]
                sc.wait_ge(ctx.gsem, bld)
                te.wait_ge(ctx.vS, gr_all)
                if acc_started:
                    te.wait_ge(ctx.vS, mix_t)
                bi4, ps4 = pool.acquire(te)
                i = te.matmul(ps4[:, 0:512], SM[0:1, 0:128],
                              SM[0:1, GR + e * 512: GR + (e + 1) * 512],
                              start=True, stop=True)
                mtgt = ctx.inc(i, "mmS", 1)
                ve.wait_ge(ctx.mmS, mtgt)
                i = ve.tensor_copy(Bb[:, 8192:8704], ps4[:, 0:512])
                gb_t = ctx.inc(i, "vS", 1)
                pool.release(bi4, "vS", gb_t)
                waited = {}
                for half in range(2):
                    w1f = {}
                    for k in range(NCH):
                        w1f[k] = ctx.ws.fetch(
                            ctx.hbm["L1_ffn_W1"][e, k * 128:(k + 1) * 128,
                                                 half * 1024:(half + 1) * 1024], 1024)
                    for fc in range(8):
                        f = half * 8 + fc
                        bi5, ps5 = pool.acquire(te)
                        i = None
                        for k in range(NCH):
                            ap, wv = w1f[k]
                            if waited.get(wv[0], 0) < wv[1]:
                                te.wait_ge(getattr(ctx, f"ws{wv[0]}"), wv[1])
                                waited[wv[0]] = wv[1]
                            i = te.matmul(ps5, ap[:, fc * 128:(fc + 1) * 128],
                                          Bb[:, k * 512:(k + 1) * 512],
                                          start=(k == 0), stop=(k == NCH - 1))
                        mtgt = ctx.inc(i, "mmS", 1)
                        sc.wait_ge(ctx.mmS, mtgt)
                        i = sc.activation(C[:, f * 512:(f + 1) * 512], ps5, AF.Relu,
                                          bias=SM[:, 342 + f:343 + f])
                        h_t = ctx.inc(i, "scS", 1)
                        pool.release(bi5, "scS", h_t)
                    ctx.ws.rel_seq(te, NCH)
                te.wait_ge(ctx.scS, ctx.cnt["scS"])
                if acc_started:
                    te.wait_ge(ctx.vS, mix_t)
                last = None
                for fk in range(16):
                    w2f, wv = ctx.ws.fetch(
                        ctx.hbm["L1_ffn_W2"][e, fk * 128:(fk + 1) * 128, :], 1024)
                    if waited.get(wv[0], 0) < wv[1]:
                        te.wait_ge(getattr(ctx, f"ws{wv[0]}"), wv[1])
                        waited[wv[0]] = wv[1]
                    for of in range(NCH):
                        last = te.matmul(accs[of], w2f[:, of * 128:(of + 1) * 128],
                                         C[:, fk * 512:(fk + 1) * 512],
                                         start=(fk == 0), stop=(fk == 15))
                    if fk == 7:
                        ctx.inc(last, "mmS", 1)
                        ctx.ws.rel_seq(te, 8)
                eo_t = ctx.inc(last, "mmS", 1)
                ctx.ws.rel_seq(te, 8)
                ve.wait_ge(ctx.mmS, eo_t)
                ve.wait_ge(ctx.vS, gb_t)
                for of in range(NCH):
                    ve.scalar_tensor_tensor(C[:, of * 512:(of + 1) * 512], accs[of],
                                            SM[:, 358 + of:359 + of], Bb[:, 8192:8704],
                                            op0=ALU.add, op1=ALU.mult)
                ve.drain()
                i = None
                if not acc_started:
                    for of in range(NCH):
                        i = ve.tensor_copy(Bb[:, 4096 + of * 512: 4096 + (of + 1) * 512],
                                           C[:, of * 512:(of + 1) * 512])
                    acc_started = True
                else:
                    for of in range(NCH):
                        i = ve.tensor_tensor(Bb[:, 4096 + of * 512: 4096 + (of + 1) * 512],
                                             Bb[:, 4096 + of * 512: 4096 + (of + 1) * 512],
                                             C[:, of * 512:(of + 1) * 512], op=ALU.add)
                mix_t = ctx.inc(i, "vS", 1)
            ve.wait_ge(ctx.vS, mix_t)
            ve.drain()
            for c in range(NCH):
                ve.tensor_tensor(Bb[:, 4096 + c * 512: 4096 + (c + 1) * 512],
                                 Bb[:, 4096 + c * 512: 4096 + (c + 1) * 512],
                                 Bb[:, c * 512:(c + 1) * 512], op=ALU.add)
            ve.drain()
            i = None
            for c in range(NCH):
                i = ve.tensor_tensor(C[:, c * 512:(c + 1) * 512],
                                     Bb[:, 4096 + c * 512: 4096 + (c + 1) * 512],
                                     Bb[:, 4096 + c * 512: 4096 + (c + 1) * 512],
                                     op=ALU.mult)
            y_t = ctx.inc(i, "vS", 1)
            te.wait_ge(ctx.vS, y_t)
            te.wait_ge(ctx.vS, ctx.cnt["vS"])
            bi6, ps6 = pool.acquire(te)
            for c in range(NCH):
                te.matmul(ps6[0:1, 0:512], ones_col,
                          Bb[:, 4096 + c * 512: 4096 + (c + 1) * 512],
                          start=(c == 0), stop=(c == NCH - 1))
            i = None
            bi7, ps7 = pool.acquire(te)
            for c in range(NCH):
                i = te.matmul(ps7[0:1, 0:512], ones_col, C[:, c * 512:(c + 1) * 512],
                              start=(c == 0), stop=(c == NCH - 1))
            mtgt = ctx.inc(i, "mmS", 1)
            ve.wait_ge(ctx.mmS, mtgt)
            ve.tensor_copy(SM[0:1, 1280:1792], ps6[0:1, 0:512])
            i = ve.tensor_copy(SM[0:1, 1792:2304][0:1, 0:512], ps7[0:1, 0:512])
            row_t = ctx.inc(i, "vS", 1)
            pool.release(bi6, "vS", row_t)
            pool.release(bi7, "vS", row_t)
            ve.drain()
            ve.tensor_scalar_mul(SM[0:1, 1280:1792], SM[0:1, 1280:1792], 1.0 / D)
            ve.drain()
            ve.tensor_tensor(SM[0:1, 4352:4864], SM[0:1, 1280:1792],
                             SM[0:1, 1280:1792], op=ALU.mult)
            ve.drain()
            ve.tensor_scalar_add(SM[0:1, 4352:4864], SM[0:1, 4352:4864], -float(L1LN_EPS))
            ve.drain()
            i = ve.scalar_tensor_tensor(SM[0:1, 4864:5376], SM[0:1, 1792:2304][0:1, 0:512],
                                        1.0 / D, SM[0:1, 4352:4864],
                                        op0=ALU.mult, op1=ALU.subtract)
            vtgt = ctx.inc(i, "vS", 1)
            sc.wait_ge(ctx.vS, vtgt)
            i = sc.activation(SM[0:1, 4864:5376], SM[0:1, 4864:5376], AF.Sqrt)
            stgt = ctx.inc(i, "scS", 1)
            ve.wait_ge(ctx.scS, stgt)
            ve.reciprocal(SM[0:1, 4864:5376], SM[0:1, 4864:5376])
            ve.drain()
            i = ve.tensor_tensor(SM[0:1, 5376:5888], SM[0:1, 1280:1792],
                                 SM[0:1, 4864:5376], op=ALU.mult)
            row2_t = ctx.inc(i, "vS", 1)
            te.wait_ge(ctx.vS, row2_t)
            bi8, ps8 = pool.acquire(te)
            te.matmul(ps8[:, 0:512], SM[0:1, 0:128], SM[0:1, 4864:5376],
                      start=True, stop=True)
            bi9, ps9 = pool.acquire(te)
            i = te.matmul(ps9[:, 0:512], SM[0:1, 0:128], SM[0:1, 5376:5888],
                          start=True, stop=True)
            mtgt = ctx.inc(i, "mmS", 1)
            ve.wait_ge(ctx.mmS, mtgt)
            if st_fence:
                ve.wait_ge(ctx.gsem, st_fence)
            for c in range(NCH):
                ve.tensor_tensor(C[:, c * 512:(c + 1) * 512],
                                 Bb[:, 4096 + c * 512: 4096 + (c + 1) * 512],
                                 ps8[:, 0:512], op=ALU.mult)
            ve.drain()
            for c in range(NCH):
                ve.tensor_tensor(C[:, c * 512:(c + 1) * 512], C[:, c * 512:(c + 1) * 512],
                                 ps9[:, 0:512], op=ALU.subtract)
            ve.drain()
            i = None
            for c in range(NCH):
                i = ve.scalar_tensor_tensor(
                    C[:, c * 512:(c + 1) * 512], C[:, c * 512:(c + 1) * 512],
                    SM[:, 326 + c:327 + c],
                    SM[:, 334 + c:335 + c].broadcast_to([128, 512]),
                    op0=ALU.mult, op1=ALU.add)
            fin_t = ctx.inc(i, "vS", 1)
            last_fin = fin_t
            pool.release(bi8, "vS", fin_t)
            pool.release(bi9, "vS", fin_t)
            gp.wait_ge(ctx.vS, fin_t)
            for c in range(NCH):
                ctx.inc(gp.dma_start(out=ffn1[b, c * 128:(c + 1) * 128, tt * 512:(tt + 1) * 512],
                                     in_=C[:, c * 512:(c + 1) * 512]), "gsem", 16)
            gfence(ctx)
            st_fence = ctx.cnt["gsem"]
    ctx.end_stage([te, ve, sc, gp])



HBM_SHAPES = {
    "x": (BPC, D, S), "xn0": (BPC, D, S), "q0": (BPC, D, S), "kv0": (BPC, 2 * D, S),
    "ao0": (BPC, D, S), "r0": (BPC, D, S),
    "ones": (128, 128), "idn": (128, 128), "rev": (128, 128),
    "band16": (16, 4095),
    "L0_attn_Wq": (D, D), "L0_attn_Wkv": (D, 2 * D), "L0_attn_Wo": (D, D),
    "L0_attn_bo": (D,),
    "fq0": (BPC, D, S), "fkv0": (BPC, 2 * D, S), "ffo0": (BPC, D, S),
    "L0_ffn_gamma1": (D,), "L0_ffn_Wq": (D, D), "L0_ffn_bq": (D,),
    "L0_ffn_Wdown": (D, 64), "L0_ffn_bdown": (64,), "L0_ffn_Wup": (64, 2 * D),
    "L0_ffn_bup": (2 * D,), "L0_ffn_Wo": (D, D), "L0_ffn_bo": (D,),
    "L0_ffn_gamma2": (D,), "L0_ffn_Wg": (D, NEXP), "L0_ffn_bg": (NEXP,),
    "L0_ffn_We": (NEXP, D, D),
    "x1": (BPC, D, S), "xn1": (BPC, D, S), "at1": (BPC, D, S),
    "ffn1": (BPC, D, S), "out": (BPC, D, S), "outs1": (BPC, D, S),
    "M_initT": (D, D), "vg_w1T": (D, D), "vg_w2T": (D, D),
    "L1_ffn_Wg": (D, NEXP), "L1_ffn_bg": (NEXP,), "L1_ffn_W1": (NEXP, D, FF),
    "L1_ffn_b1": (NEXP, FF), "L1_ffn_W2": (NEXP, FF, D), "L1_ffn_b2": (NEXP, D),
    "L1_ffn_gamma": (D,), "L1_ffn_beta": (D,),
    "L1_attn_eta_w": (D,), "L1_attn_eta_b": (1,), "L1_attn_alpha_w": (D,),
    "L1_attn_alpha_b": (1,), "L1_attn_gate_w": (D,), "L1_attn_gate_b": (1,),
}

STAGES = [
    ("ln0", lambda c: st_layernorm(c, "x", "xn0", LN_EPS),
     ["x", "ones"], ["xn0"]),
    ("qkv0", st_qkv0, ["xn0", "L0_attn_Wq", "L0_attn_Wkv"], ["q0", "kv0"]),
    ("attn0", st_attn0, ["q0", "kv0", "band16", "idn", "rev"], ["ao0"]),
    ("awo0", lambda c: st_addproj(c, "ao0", "x", "L0_attn_Wo", "L0_attn_bo", "r0"),
     ["ao0", "x", "L0_attn_Wo", "L0_attn_bo"], ["r0"]),
    ("ffn0a", st_ffn0a,
     ["xn0", "ones", "L0_ffn_gamma1", "L0_ffn_Wq", "L0_ffn_bq", "L0_ffn_Wdown",
      "L0_ffn_bdown", "L0_ffn_Wup", "L0_ffn_bup"], ["fq0", "fkv0"]),
    ("ffn0b", st_ffn0b, ["fq0", "fkv0", "idn"], ["ffo0"]),
    ("ffn0c", st_ffn0c,
     ["ffo0", "xn0", "r0", "ones", "idn", "L0_ffn_Wo", "L0_ffn_bo", "L0_ffn_gamma2",
      "L0_ffn_Wg", "L0_ffn_bg", "L0_ffn_We"], ["x1"]),
    ("ln1", lambda c: st_layernorm(c, "x1", "xn1", LN_EPS), ["x1", "ones"], ["xn1"]),
    ("scan1", st_scan1,
     ["xn1", "ones", "idn", "M_initT", "vg_w1T", "vg_w2T", "L1_attn_eta_w", "L1_attn_eta_b",
      "L1_attn_alpha_w", "L1_attn_alpha_b", "L1_attn_gate_w", "L1_attn_gate_b"],
     ["outs1"]),
    ("vg1", st_vg1, ["outs1", "vg_w1T", "vg_w2T"], ["at1"]),
    ("ffn1", st_ffn1,
     ["xn1", "ones", "idn", "L1_ffn_Wg", "L1_ffn_bg", "L1_ffn_W1", "L1_ffn_b1",
      "L1_ffn_W2", "L1_ffn_b2", "L1_ffn_gamma", "L1_ffn_beta"], ["ffn1"]),
    ("fin", st_fin, ["x1", "at1", "ffn1"], ["out"]),
]


def build_group(stages, group_inputs, group_outputs):
    ctx = Ctx("+".join(s[0] for s in stages))
    for name in group_inputs:
        ctx.din(name, HBM_SHAPES[name])
    for name in group_outputs:
        ctx.dout(name, HBM_SHAPES[name])
    ctx.alloc()
    for _, fn, _, _ in stages:
        fn(ctx)
    ctx.wait(ctx.gp, "gsem")
    ctx.close()
    mybir.codegen_inst_isa_subclasses(ctx.nc)
    return ctx


def host_prep(inputs):
    x = np.asarray(inputs["L0_attn_x"])
    x_fm = np.ascontiguousarray(np.transpose(x, (0, 2, 1)))
    consts = {
        "ones": np.ones((128, 128), np.float32),
        "M_initT": np.ascontiguousarray(np.asarray(inputs["L1_attn_M_init"]).T),
        "vg_w1T": np.ascontiguousarray(np.asarray(inputs["L1_attn_vg_w1"]).T),
        "vg_w2T": np.ascontiguousarray(np.asarray(inputs["L1_attn_vg_w2"]).T),
        "idn": np.eye(128, dtype=np.float32),
        "rev": np.eye(128, dtype=np.float32)[::-1].copy(),
        "band16": _host_band16(inputs["L0_attn_rel_emb"]).astype(np.float32),
    }
    return x_fm, consts


def run_stages(inputs, upto=None):
    stages = STAGES if upto is None else STAGES[:upto]
    x_fm, consts = host_prep(inputs)
    core_vals = []
    for ci in range(NCORE):
        d = dict(consts)
        d["x"] = np.ascontiguousarray(x_fm[ci * BPC:(ci + 1) * BPC])
        for k in ["L0_attn_Wq", "L0_attn_Wkv", "L0_attn_Wo", "L0_attn_bo",
                  "L0_ffn_gamma1", "L0_ffn_Wq", "L0_ffn_bq", "L0_ffn_Wdown",
                  "L0_ffn_bdown", "L0_ffn_Wup", "L0_ffn_bup", "L0_ffn_Wo",
                  "L0_ffn_bo", "L0_ffn_gamma2", "L0_ffn_Wg", "L0_ffn_bg",
                  "L0_ffn_We", "L1_attn_eta_w", "L1_attn_eta_b", "L1_attn_alpha_w",
                  "L1_attn_alpha_b", "L1_attn_gate_w", "L1_attn_gate_b",
                  "L1_ffn_Wg", "L1_ffn_bg", "L1_ffn_W1", "L1_ffn_b1", "L1_ffn_W2",
                  "L1_ffn_b2", "L1_ffn_gamma", "L1_ffn_beta"]:
            d[k] = np.ascontiguousarray(np.asarray(inputs[k]))
        core_vals.append(d)
    for (name, fn, ins, outs) in stages:
        ctx = build_group([(name, fn, ins, outs)], ins, outs)
        in_maps = [{k: core_vals[ci][k] for k in ins} for ci in range(NCORE)]
        res = run_bass_kernel_spmd(ctx.nc, in_maps, core_ids=list(range(NCORE)))
        for ci in range(NCORE):
            for k in outs:
                core_vals[ci][k] = np.asarray(res.results[ci][k])
    return core_vals


def kernel(**inputs):
    core_vals = run_stages(inputs)
    outs = [np.transpose(core_vals[ci]["out"], (0, 2, 1)) for ci in range(NCORE)]
    return np.ascontiguousarray(np.concatenate(outs, axis=0)).astype(np.float32)
